# version 07
# baseline (speedup 1.0000x reference)
import contextlib
import concourse.bass as bass
import concourse.mybir as mybir

F32 = mybir.dt.float32
BF16 = mybir.dt.bfloat16
AF = mybir.ActivationFunctionType
ALU = mybir.AluOpType
AX = mybir.AxisListType

B, NSEQ, D = 4, 8192, 1024
H, HD = 16, 64
RG, XG = 32, 256
FF = 4096
NE = 9
NTB, TQ = 4, 1024
TLOC = NTB * TQ
NKT = D // 128
NT = TQ // 128
G_F = 4
NGRP = FF // 128 // G_F
NHP = H // 2

ENGS = ("tensor", "vector", "scalar", "gpsimd", "sync")


def build(dbg=()):
    nc = bass.Bass("TRN2", target_bir_lowering=False, debug=False)

    xb = nc.dram_tensor("xb", [NSEQ, D], F32, kind="ExternalInput").ap()
    wqkv = nc.dram_tensor("wqkv", [3 * D, D], F32, kind="ExternalInput").ap()
    wproj = nc.dram_tensor("wproj", [D, D], F32, kind="ExternalInput").ap()
    gw = nc.dram_tensor("gw", [8, D], F32, kind="ExternalInput").ap()
    gb = nc.dram_tensor("gb", [1, 8], F32, kind="ExternalInput").ap()
    wu_in = nc.dram_tensor("wu", [8, FF, D], F32, kind="ExternalInput").ap()
    bu_in = nc.dram_tensor("bu", [8, FF], F32, kind="ExternalInput").ap()
    wd_in = nc.dram_tensor("wd", [8, D, FF], F32, kind="ExternalInput").ap()
    bd_in = nc.dram_tensor("bd", [8, D], F32, kind="ExternalInput").ap()
    swu_in = nc.dram_tensor("swu", [FF, D], F32, kind="ExternalInput").ap()
    sbu_in = nc.dram_tensor("sbu", [1, FF], F32, kind="ExternalInput").ap()
    swd_in = nc.dram_tensor("swd", [D, FF], F32, kind="ExternalInput").ap()
    sbd_in = nc.dram_tensor("sbd", [1, D], F32, kind="ExternalInput").ap()

    yout = nc.dram_tensor("yout", [TLOC, D], F32, kind="ExternalOutput").ap()

    s_wq_bf = nc.dram_tensor("s_wq_bf", [3 * D, D], BF16, kind="Internal").ap()
    s_wp_bf = nc.dram_tensor("s_wp_bf", [D, D], BF16, kind="Internal").ap()
    s_wu_bf = nc.dram_tensor("s_wu_bf", [FF, D], BF16, kind="Internal").ap()
    s_wd_bf = nc.dram_tensor("s_wd_bf", [D, FF], BF16, kind="Internal").ap()
    s_wu_bf2 = nc.dram_tensor("s_wu_bf2", [2, NKT, FF, 128], BF16, kind="Internal").ap()
    s_wd_bf2 = nc.dram_tensor("s_wd_bf2", [2, FF // 128, D, 128], BF16, kind="Internal").ap()
    s_wqT = nc.dram_tensor("s_wqT", [D, 3 * D], BF16, kind="Internal").ap()
    s_wpT = nc.dram_tensor("s_wpT", [D, D], BF16, kind="Internal").ap()
    s_wuT = nc.dram_tensor("s_wuT", [NE, 128, FF * NKT], BF16, kind="Internal").ap()
    s_wdT = nc.dram_tensor("s_wdT", [NE, FF, D], BF16, kind="Internal").ap()
    s_xn_tok = nc.dram_tensor("s_xn_tok", [NSEQ, D], BF16, kind="Internal").ap()
    s_xn_fm = nc.dram_tensor("s_xn_fm", [D, NSEQ], BF16, kind="Internal").ap()
    s_kfm = nc.dram_tensor("s_kfm", [NHP, 128, NSEQ], BF16, kind="Internal").ap()
    s_vtok = nc.dram_tensor("s_vtok", [NSEQ, D], BF16, kind="Internal").ap()
    s_otok = nc.dram_tensor("s_otok", [TQ, D], BF16, kind="Internal").ap()

    dbg_out = {}
    def _dbg(name, shape):
        if name in dbg:
            dbg_out[name] = nc.dram_tensor("d_" + name, shape, F32, kind="ExternalOutput").ap()
    _dbg("xn", [NSEQ, D])
    _dbg("xnfm", [D, NSEQ])
    _dbg("wqt", [D, 3 * D])
    _dbg("kfm", [NHP, 128, NSEQ])
    _dbg("vtok", [NSEQ, D])
    _dbg("otok", [NTB, TQ, D])
    _dbg("w", [NTB, 128, NT, NE])
    _dbg("q0", [128, TQ])
    _dbg("wut0", [128, FF * NKT])
    _dbg("combe", [NE + 1, 128, NT, D])
    _dbg("a10", [128, G_F, 512])
    _dbg("wug0", [128, G_F * NKT * 128])
    _dbg("wdt0", [FF, D])
    _dbg("wq0", [128, NKT, 128])
    _dbg("xtb0", [128, NKT, TQ])
    _dbg("a0", [64, XG])
    _dbg("at0", [128, 2, RG])
    _dbg("o0", [32, RG * HD])
    _dbg("sexp0", [64, XG])
    _dbg("comb1", [NTB, 128, NT, D])

    es = contextlib.ExitStack()
    def sbt(name, shape, dt):
        return es.enter_context(nc.sbuf_tensor(name, shape, dt))
    def pst(name, shape, dt):
        return es.enter_context(nc.psum_tensor(name, shape, dt))

    combined = sbt("combined", [128, NT, D], F32)
    xn_tb = sbt("xn_tb", [128, NKT, TQ], BF16)
    tstage = sbt("tstage", [128, 2, 2048], BF16)
    kpair = sbt("kpair", [128, 2, NSEQ], BF16)
    qpair = sbt("qpair", [128, TQ], BF16)
    vgrp = sbt("vgrp", [128, 2, RG, 128], BF16)
    sexp = sbt("sexp", [64, XG], F32)
    smax = sbt("smax", [64, 4], F32)
    abf = sbt("abf", [64, XG], BF16)
    atr = sbt("atr", [128, 2, RG], BF16)
    osb = sbt("osb", [32, RG * HD], BF16)
    ofm = sbt("ofm", [128, NKT, TQ], BF16)
    wpT = sbt("wpT", [128, NKT, 512], BF16)
    xnstr = sbt("xnstr", [128, 2, NKT, 256], BF16)
    kvst = sbt("kvst", [128, 2, 512], BF16)
    vstage = sbt("vstage", [128, 2, D], BF16)
    wqTq = sbt("wqTq", [128, 2, NKT, 128], BF16)
    scst = sbt("scst", [128, NT, 8], F32)
    gtmp = sbt("gtmp", [128, NT, 8], F32)
    gtmp2 = sbt("gtmp2", [128, NT, 8], F32)
    gred = sbt("gred", [128, NT, 4], F32)
    wgate = sbt("wgate", [128, NT, NE], F32)
    wgate_bf = sbt("wgate_bf", [128, NT, NE], BF16)
    wT = sbt("wT", [NE, TQ], BF16)
    bdmat = sbt("bdmat", [NE, D], BF16)
    gbias = sbt("gbias", [1, 8], BF16)
    gw_bf = sbt("gw_bf", [8, D], BF16)
    gwT = sbt("gwT", [128, NKT, 8], BF16)
    ident = sbt("ident", [128, 128], BF16)
    ones_bf = sbt("ones_bf", [1, 128], BF16)
    stat = sbt("stat", [128, 8], F32)
    wug = sbt("wug", [128, 2, G_F * NKT * 128], BF16)
    wdg = sbt("wdg", [128, 2, G_F, D], BF16)
    xch = combined[:, 0:2, :]
    xcent = combined[:, 2, :]
    xsq = combined[:, 3, :]
    xnout = combined[:, 4:5, :].bitcast(BF16).rearrange("p o (a b) -> p (o a) b", a=2)
    a1g = sbt("a1g", [128, G_F, 512], BF16)
    bug9 = sbt("bug9", [128, NE, FF // 128], F32)
    bustage = sbt("bustage", [32, 128], F32)
    identf = sbt("identf", [128, 128], F32)
    xtile = sbt("xtile", [128, 2, 512], F32)

    wqTk_v = wug[:].rearrange("p b (j k c) -> p (b j) k c", k=NKT, c=128)
    wqTv_v = wdg[:].rearrange("p b j d -> p (b j) d")

    ps_a = pst("ps_a", [128, 2, 512], F32)
    ps_y = pst("ps_y", [128, 2, 2, 512], F32)
    ps_s = pst("ps_s", [128, 512], F32)
    ps_t = pst("ps_t", [128, 1024], BF16)

    SEMNAMES = ("mm", "dv", "ac", "gc", "sq", "aq", "gq",
                "tsti0", "tsti1", "tsto0", "tsto1", "sxp", "scu", "scd", "skv",
                "sot", "sof", "sxn", "swq0", "swq1", "skp", "svg0", "svg1",
                "sxs0", "sxs1", "swu0", "swu1", "swd0", "swd1", "swp",
                "sxt0", "sxt1", "sxc0", "sxc1", "sbu", "smisc", "syo", "sdbg")
    sems = {s: es.enter_context(nc.semaphore(s)) for s in SEMNAMES}
    cnt = {s: 0 for s in sems}
    SCHED = {e: [] for e in ENGS}
    last_wait = {e: {} for e in ENGS}

    def op(eng, fn, waits=(), inc=None, k=1):
        wl = []
        for wv in waits:
            if wv is None:
                continue
            sname, val = wv
            if val <= 0:
                continue
            if val > last_wait[eng].get(sname, -1):
                wl.append((sname, val))
                last_wait[eng][sname] = val
        SCHED[eng].append((tuple(wl), fn, inc, k))
        if inc is not None:
            cnt[inc] += k
            return (inc, cnt[inc])
        return None

    def dma(eng, sem, out_ap, in_ap, waits=(), **kw):
        def mk(o, i, kw2):
            return lambda E: E.dma_start(out=o, in_=i, **kw2)
        return op(eng, mk(out_ap, in_ap, kw), waits=waits, inc=sem, k=16)

    def dmaT(eng, sem, out_ap, in_ap, waits=()):
        def mk(o, i):
            return lambda E: E.dma_start_transpose(o, i)
        return op(eng, mk(out_ap, in_ap), waits=waits, inc=sem, k=16)

    def C(s):
        return (s, cnt[s])

    bguard = {}
    def BG(key):
        return bguard.get(key)

    def CT():
        return [("tsto0", cnt["tsto0"]), ("tsto1", cnt["tsto1"])]

    rr = [0]
    tst_guard = [None, None]
    last_t1 = [None]

    def transpose_stage(src_ap, dst_ap, ncols, waits):
        par = rr[0] % 2
        rr[0] += 1
        w = list(waits) + [tst_guard[par]]
        t1 = dmaT("sync", "tsti%d" % par, tstage[:, par, 0:ncols], src_ap, waits=w)
        last_t1[0] = t1
        t2 = dma("gpsimd", "tsto%d" % par, dst_ap, tstage[:, par, 0:ncols], waits=[t1])
        tst_guard[par] = t2
        return t2

    i0 = op("gpsimd", lambda E: E.memset(ident[:], 1.0), inc="gc")
    ident_done = op("gpsimd", lambda E: E.affine_select(ident[:], ident[:], [[1, 128]],
                                                        ALU.is_equal, 0.0, base=0,
                                                        channel_multiplier=-1),
                    waits=[i0], inc="gc")
    op("gpsimd", lambda E: E.memset(ones_bf[:], 1.0), inc="gc")
    ones_done = C("gc")
    dma("gpsimd", "smisc", gbias[:], gb)
    dma("gpsimd", "smisc", bdmat[0:8, :], bd_in)
    dma("gpsimd", "smisc", bdmat[8:9, :], sbd_in)
    dma("gpsimd", "smisc", gw_bf[:], gw)
    misc_done = C("smisc")
    bdm_done = misc_done
    cgw = misc_done

    idf = op("vector", lambda E: E.tensor_copy(identf[:], ident[:]), waits=[ident_done], inc="dv")
    bu_done = None
    for e in range(NE):
        _src = (bu_in[e:e + 1, :] if e < 8 else sbu_in).rearrange("o (j p) -> (o j) p", p=128)
        bl = dma("gpsimd", "sbu", bustage[:], _src, waits=[bu_done])
        def mkbt(ee):
            return lambda E: E.transpose(ps_s[0:128, 256:288].bitcast(F32) if False else ps_s[0:128, 256:288],
                                         bustage[:], identf[0:32, 0:32])
        mt = op("tensor", mkbt(e), waits=[bl, idf, BG("s")], inc="mm")
        def mkbc(ee):
            return lambda E: E.tensor_copy(bug9[:, ee, :], ps_s[0:128, 256:288])
        bu_done = op("vector", mkbc(e), waits=[mt], inc="dv")
        bguard["s"] = bu_done
    cw = dma("gpsimd", "scu", s_wq_bf, wqkv)
    for k in range(NKT):
        for hh in range(2):
            transpose_stage(s_wq_bf[hh * 1536:(hh + 1) * 1536, k * 128:(k + 1) * 128],
                            s_wqT[k * 128:(k + 1) * 128, hh * 1536:(hh + 1) * 1536],
                            1536, [cw])
    wqT_done = CT()
    cw = dma("gpsimd", "scu", s_wp_bf, wproj, waits=wqT_done)
    for k in range(NKT):
        for hh in range(2):
            transpose_stage(s_wp_bf[hh * 512:(hh + 1) * 512, k * 128:(k + 1) * 128],
                            s_wpT[k * 128:(k + 1) * 128, hh * 512:(hh + 1) * 512],
                            512, [cw])
    wpT_done = CT()
    for k in range(NKT):
        def mk_gwt(kk):
            return lambda E: E.transpose(ps_t[0:128, 0:8], gw_bf[0:8, kk * 128:(kk + 1) * 128],
                                         ident[0:8, 0:8])
        m = op("tensor", mk_gwt(k), waits=[cgw, ident_done, BG("t")], inc="mm")
        def mk_gwc(kk):
            return lambda E: E.tensor_copy(gwT[:, kk, :], ps_t[0:128, 0:8])
        bguard["t"] = op("vector", mk_gwc(k), waits=[m], inc="dv")
    gwT_done = C("dv")

    wu_ready = [None] * NE
    wd_ready = [None] * NE
    wuT_view = s_wuT.rearrange("e p (t k c) -> e p t k c", k=NKT, c=128)

    def prep_expert(e):
        cu = dma("gpsimd", "scu", s_wu_bf, (wu_in[e] if e < 8 else swu_in),
                 waits=(wu_ready[e - 1] if e > 0 else wpT_done))
        cd = dma("gpsimd", "scd", s_wd_bf, (wd_in[e] if e < 8 else swd_in),
                 waits=(wd_ready[e - 1] if e > 0 else []))
        for k in range(NKT):
            for hh in range(2):
                dst = wuT_view[e][:, hh * 16:(hh + 1) * 16, k, :]
                srca = s_wu_bf[hh * 2048:(hh + 1) * 2048, k * 128:(k + 1) * 128]
                transpose_stage(srca, dst, 2048, [cu])
        wu_ready[e] = CT()
        for f in range(FF // 128):
            transpose_stage(s_wd_bf[:, f * 128:(f + 1) * 128],
                            s_wdT[e][f * 128:(f + 1) * 128, :], D, [cd])
        wd_ready[e] = CT()
    prep_queue = list(range(NE))

    def pump_prep(n=1):
        for _ in range(n):
            if prep_queue:
                prep_expert(prep_queue.pop(0))

    xch_guard = [None, None]
    xn_last = None
    for i in range(NSEQ // 128):
        par = i % 2
        ld = dma("sync", "sxc%d" % par, xch[:, par, :], xb[i * 128:(i + 1) * 128, :],
                 waits=[xch_guard[par]])
        def _r1(p):
            return lambda E: E.reduce_sum(stat[:, 0:1], xch[:, p, :], axis=AX.X)
        r1 = op("vector", _r1(par), waits=[ld, C("dv")], inc="dv")
        r2 = op("vector", lambda E: E.tensor_scalar(stat[:, 1:2], stat[:, 0:1], 1.0 / D, None, ALU.mult),
                waits=[r1], inc="dv")
        def _r3(p):
            return lambda E: E.scalar_tensor_tensor(xcent, xch[:, p, :], stat[:, 1:2],
                                                    xch[:, p, :], ALU.subtract, ALU.bypass)
        r3 = op("vector", _r3(par), waits=[r2], inc="dv")
        xch_guard[par] = r3
        r4 = op("scalar", lambda E: E.activation(xsq, xcent, AF.Square), waits=[r3, C("ac")], inc="ac")
        r5 = op("vector", lambda E: E.reduce_sum(stat[:, 2:3], xsq, axis=AX.X), waits=[r4], inc="dv")
        r6 = op("vector", lambda E: E.tensor_scalar(stat[:, 3:4], stat[:, 2:3], 1.0 / D, 1.0e-5,
                                                    ALU.mult, ALU.add), waits=[r5], inc="dv")
        r7 = op("scalar", lambda E: E.activation(stat[:, 4:5], stat[:, 3:4], AF.Sqrt), waits=[r6], inc="ac")
        r8 = op("vector", lambda E: E.reciprocal(stat[:, 5:6], stat[:, 4:5]), waits=[r7], inc="dv")
        def _r9(p):
            return lambda E: E.scalar_tensor_tensor(xnout[:, p, :], xcent, stat[:, 5:6],
                                                    xcent, ALU.mult, ALU.bypass)
        r9 = op("vector", _r9(par), waits=[r8], inc="dv")
        xn_last = dma("gpsimd", "sxp", s_xn_tok[i * 128:(i + 1) * 128, :], xnout[:, par, :], waits=[r9])
    xn_tok_done = xn_last
    ln_done = [("sxp", 16 * 16 * (ch + 1)) for ch in range(4)]

    if "xn" in dbg_out:
        for i in range(NSEQ // 128):
            ldb = dma("sync", "sq", xnout[:, 0, :], s_xn_tok[i * 128:(i + 1) * 128, :],
                      waits=[xn_tok_done, C("dv")])
            cv = op("vector", lambda E: E.tensor_copy(xcent, xnout[:, 0, :]), waits=[ldb], inc="dv")
            dma("sync", "sq", dbg_out["xn"][i * 128:(i + 1) * 128, :], xcent, waits=[cv])

    xnq_done = []
    for ch in range(4):
        for k in range(NKT):
            transpose_stage(
                s_xn_tok[ch * 2048:(ch + 1) * 2048, k * 128:(k + 1) * 128],
                s_xn_fm[k * 128:(k + 1) * 128, ch * 2048:(ch + 1) * 2048],
                2048, [ln_done[ch]])
        xnq_done.append(CT())
    xnfm_done = CT()

    pump_prep(2)
    if "xnfm" in dbg_out:
        for i in range(D // 128):
            ldb = dma("sync", "sq", xnout[:, 0, :].bitcast(BF16) if False else xnout[:, 0, :],
                      s_xn_fm[i * 128:(i + 1) * 128, 0:D], waits=xnfm_done + [C("dv")])
            cv = op("vector", lambda E: E.tensor_copy(xcent, xnout[:, 0, :]), waits=[ldb], inc="dv")
            d2 = dma("sync", "sq", dbg_out["xnfm"][i * 128:(i + 1) * 128, 0:D], xcent, waits=[cv])
            for ch in range(1, NSEQ // D):
                ldb = dma("sync", "sq", xnout[:, 0, :], s_xn_fm[i * 128:(i + 1) * 128, ch * D:(ch + 1) * D],
                          waits=[d2, C("dv")])
                cv = op("vector", lambda E: E.tensor_copy(xcent, xnout[:, 0, :]), waits=[ldb], inc="dv")
                d2 = dma("sync", "sq", dbg_out["xnfm"][i * 128:(i + 1) * 128, ch * D:(ch + 1) * D],
                         xcent, waits=[cv])
    if "wqt" in dbg_out:
        for i in range(D // 128):
            d2 = None
            for ch in range(3):
                ldb = dma("sync", "sq", xnout[:, 0, :], s_wqT[i * 128:(i + 1) * 128, ch * D:(ch + 1) * D],
                          waits=wqT_done + [d2, C("dv")])
                cv = op("vector", lambda E: E.tensor_copy(xcent, xnout[:, 0, :]), waits=[ldb], inc="dv")
                d2 = dma("sync", "sq", dbg_out["wqt"][i * 128:(i + 1) * 128, ch * D:(ch + 1) * D],
                         xcent, waits=[cv])
    wug_flat = wug[:].rearrange("p b f -> p (b f)")
    l_k = None
    for g_ in range(NHP):
        l_k = dma("scalar", "swu0",
                  wug_flat[:, g_ * (NKT * 128):(g_ + 1) * (NKT * 128)]
                  .rearrange("p (k c) -> p k c", c=128),
                  s_wqT[:, D + g_ * 128:D + (g_ + 1) * 128].rearrange("(k p) c -> p k c", p=128),
                  waits=wqT_done)
    l_v = dma("scalar", "swd0", wqTv_v,
              s_wqT[:, 2 * D:3 * D].rearrange("(k p) v -> p k v", p=128),
              waits=wqT_done)

    xnstr_guard = [None, None]
    kvev_guard = [None, None]
    vst_guard = [None, None]
    wug_guard = [None, None]
    wdg_guard = [None, None]
    for tcx in range(NSEQ // 256):
        par = tcx % 2
        ldx = dma("scalar", "sxs%d" % par, xnstr[:, par, :, :],
                  s_xn_fm[:, tcx * 256:(tcx + 1) * 256].rearrange("(k p) t -> p k t", p=128),
                  waits=xnq_done[tcx // 8] + [xnstr_guard[par]])
        last_consumer = None
        for g in range(NHP):
            mi = None
            for k in range(NKT):
                def mk(gg, kk, p):
                    return lambda E: E.matmul(ps_a[:, gg % 2, 0:256], wqTk_v[:, gg, kk, :],
                                              xnstr[:, p, kk, :],
                                              start=(kk == 0), stop=(kk == NKT - 1))
                mi = op("tensor", mk(g, k, par),
                        waits=[ldx, l_k, BG("a%d" % (g % 2))] if k == 0 else (),
                        inc="mm" if k == NKT - 1 else None)
            def mkev(gg):
                return lambda E: E.activation(kvst[:, gg % 2, 0:256], ps_a[:, gg % 2, 0:256], AF.Copy)
            ev = op("scalar", mkev(g), waits=[mi, kvev_guard[g % 2]], inc="ac")
            bguard["a%d" % (g % 2)] = ev
            sp = dma("scalar", "skv", s_kfm[g][:, tcx * 256:(tcx + 1) * 256], kvst[:, g % 2, 0:256],
                     waits=[ev])
            kvev_guard[g % 2] = sp
            last_consumer = mi
        for ts in range(2):
            evs = []
            for half in range(2):
                mi = None
                for k in range(NKT):
                    def mkv(hh, kk, p, t):
                        return lambda E: E.matmul(ps_y[:, t, hh, :],
                                                  xnstr[:, p, kk, t * 128:(t + 1) * 128],
                                                  wqTv_v[:, kk, hh * 512:(hh + 1) * 512],
                                                  start=(kk == 0), stop=(kk == NKT - 1))
                    mi = op("tensor", mkv(half, k, par, ts),
                            waits=[ldx, l_v, BG("y%d0" % ts), BG("y%d1" % ts)]
                            if (k == 0 and half == 0) else (),
                            inc="mm" if k == NKT - 1 else None)
                def mkve(hh, t):
                    return lambda E: E.activation(vstage[:, t, hh * 512:(hh + 1) * 512],
                                                  ps_y[:, t, hh, :], AF.Copy)
                ev_v = op("scalar", mkve(half, ts), waits=[mi, vst_guard[ts]], inc="ac")
                bguard["y%d%d" % (ts, half)] = ev_v
                evs.append(ev_v)
            sp = dma("scalar", "skv", s_vtok[tcx * 256 + ts * 128: tcx * 256 + (ts + 1) * 128, :],
                     vstage[:, ts, :], waits=evs)
            vst_guard[ts] = sp
            last_consumer = mi
        xnstr_guard[par] = last_consumer
    kv_done_all = [C("skv")]
    wug_guard[0] = wug_guard[1] = C("mm")
    wdg_guard[0] = wdg_guard[1] = C("mm")

    if "kfm" in dbg_out:
        for g in range(NHP):
            for ch in range(NSEQ // 512):
                ldb = dma("sync", "sq", kvst[:, 0, :], s_kfm[g][:, ch * 512:(ch + 1) * 512],
                          waits=kv_done_all + [C("dv")])
                cv = op("vector", lambda E: E.tensor_copy(xsq[:, 0:512], kvst[:, 0, :]),
                        waits=[ldb], inc="dv")
                dma("sync", "sq", dbg_out["kfm"][g][:, ch * 512:(ch + 1) * 512], xsq[:, 0:512],
                    waits=[cv])
    if "vtok" in dbg_out:
        for i in range(NSEQ // 128):
            ldb = dma("sync", "sq", vstage[:, 0, :], s_vtok[i * 128:(i + 1) * 128, :],
                      waits=kv_done_all + [C("dv"), C("ac")])
            cv = op("vector", lambda E: E.tensor_copy(xsq, vstage[:, 0, :]), waits=[ldb], inc="dv")
            dma("sync", "sq", dbg_out["vtok"][i * 128:(i + 1) * 128, :], xsq, waits=[cv])

    xn_tb_guard = [None]
    comb_store = [None]
    kp_guard = [None, None]
    q_guard = [None]
    vg_guard = [None, None]
    wq_guard = [None, None]
    osb_guard = [None]
    ofm_guard = [None]
    wpT_guard = [None]
    xt_guard = [None, None]
    a1_guard = [None] * G_F

    for tb in range(NTB):
        t0 = tb * TQ

        ld_xn = dma("scalar", "sxn", xn_tb[:],
                    s_xn_fm[:, t0:t0 + TQ].rearrange("(k p) t -> p k t", p=128),
                    waits=xnq_done[tb // 2] + [xn_tb_guard[0]])

        for tt in range(NT):
            def mkb(t):
                return lambda E: E.matmul(ps_a[:, t % 2, 0:8], ones_bf[0:1, 0:128], gbias[0:1, :],
                                          start=True, stop=False)
            op("tensor", mkb(tt), waits=[ld_xn, gwT_done, ones_done, BG("a%d" % (tt % 2))])
            mi = None
            for k in range(NKT):
                def mkg(t, kk):
                    return lambda E: E.matmul(ps_a[:, t % 2, 0:8], xn_tb[:, kk, t * 128:(t + 1) * 128],
                                              gwT[:, kk, :], start=False, stop=(kk == NKT - 1))
                mi = op("tensor", mkg(tt, k), inc="mm" if k == NKT - 1 else None)
            def mks(t):
                return lambda E: E.activation(scst[:, t, :], ps_a[:, t % 2, 0:8], AF.Sigmoid)
            bguard["a%d" % (tt % 2)] = op("scalar", mks(tt), waits=[mi], inc="ac")
        gate_sc = C("ac")

        def bget(apc):
            return apc.rearrange("p (g o) -> p g o", o=1).broadcast_to([128, NT, 8])
        g1 = op("vector", lambda E: E.reduce_max(gred[:, :, 0], scst[:], axis=AX.X),
                waits=[gate_sc, C("dv")], inc="dv")
        g2 = op("vector", lambda E: E.tensor_tensor(gtmp[:], scst[:], bget(gred[:, :, 0]), ALU.is_equal),
                waits=[g1], inc="dv")
        g3 = op("vector", lambda E: E.scalar_tensor_tensor(gtmp2[:], gtmp[:], -1.0e30, scst[:],
                                                           ALU.mult, ALU.add), waits=[g2], inc="dv")
        g4 = op("vector", lambda E: E.reduce_max(gred[:, :, 1], gtmp2[:], axis=AX.X), waits=[g3], inc="dv")
        g5 = op("vector", lambda E: E.tensor_tensor(gtmp[:], scst[:], bget(gred[:, :, 1]), ALU.is_ge),
                waits=[g4], inc="dv")
        g6 = op("vector", lambda E: E.tensor_tensor(gred[:, :, 2], gred[:, :, 0], gred[:, :, 1], ALU.add),
                waits=[g5], inc="dv")
        g7 = op("vector", lambda E: E.tensor_scalar(gred[:, :, 2], gred[:, :, 2], 1.0e-6, None, ALU.add),
                waits=[g6], inc="dv")
        g8 = op("vector", lambda E: E.reciprocal(gred[:, :, 3], gred[:, :, 2]), waits=[g7], inc="dv")
        g9 = op("vector", lambda E: E.tensor_tensor(gtmp2[:], scst[:], gtmp[:], ALU.mult), waits=[g8], inc="dv")
        g10 = op("vector", lambda E: E.tensor_tensor(wgate[:, :, 0:8], gtmp2[:], bget(gred[:, :, 3]), ALU.mult),
                 waits=[g9], inc="dv")
        g11 = op("vector", lambda E: E.memset(wgate[:, :, 8:9], 0.1), waits=[g10], inc="dv")
        g12 = op("vector", lambda E: E.tensor_copy(wgate_bf[:], wgate[:]), waits=[g11], inc="dv")
        wgate_done = g11
        if "w" in dbg_out:
            dma("sync", "sq", dbg_out["w"][tb], wgate[:], waits=[wgate_done])
        wT_last = None
        for tt in range(NT):
            def mkt(t):
                return lambda E: E.transpose(ps_t[0:NE, 128:256], wgate_bf[:, t, :], ident[:])
            m = op("tensor", mkt(tt), waits=[g12, BG("t"), ident_done], inc="mm")
            def mkc(t):
                return lambda E: E.tensor_copy(wT[0:NE, t * 128:(t + 1) * 128], ps_t[0:NE, 128:256])
            wT_last = op("vector", mkc(tt), waits=[m], inc="dv")
            bguard["t"] = wT_last
        wT_done = wT_last

        for g in range(NHP):
            kld = dma("scalar", "skp", kpair[:, g % 2, :], s_kfm[g],
                      waits=kv_done_all + [kp_guard[g % 2]])
            qw = dma("scalar", "swq%d" % (g % 2), wqTq[:, g % 2, :, :],
                     s_wqT[:, g * 128:(g + 1) * 128].rearrange("(k p) c -> p k c", p=128),
                     waits=wqT_done + [wq_guard[g % 2]])
            q_ready = None
            for ch in range(TQ // 512):
                mi = None
                for k in range(NKT):
                    def mkq(gg, kk, cc):
                        return lambda E: E.matmul(ps_a[:, cc % 2, :],
                                                  wqTq[:, gg % 2, kk, :],
                                                  xn_tb[:, kk, cc * 512:(cc + 1) * 512],
                                                  start=(kk == 0), stop=(kk == NKT - 1))
                    mi = op("tensor", mkq(g, k, ch),
                            waits=[qw, ld_xn, q_guard[0], BG("a%d" % (ch % 2))] if k == 0 else (),
                            inc="mm" if k == NKT - 1 else None)
                wq_guard[g % 2] = mi
                def mkqe(cc):
                    return lambda E: E.activation(qpair[:, cc * 512:(cc + 1) * 512], ps_a[:, cc % 2, :],
                                                  AF.Copy, scale=0.125)
                q_ready = op("scalar", mkqe(ch), waits=[mi], inc="ac")
                bguard["a%d" % (ch % 2)] = q_ready

            if tb == 0 and g >= 1:
                pump_prep(1)
            vld_pair = None
            for xt in range(2):
                vld_pair = dma("scalar", "svg0",
                               vgrp[:, xt, :, :],
                               s_vtok[xt * 4096:(xt + 1) * 4096, g * 128:(g + 1) * 128]
                               .rearrange("(x r) c -> x r c", r=RG),
                               waits=(kv_done_all + [vg_guard[0]]) if xt == 0 else ())
            for hh in range(2):
                head = 2 * g + hh
                hb = hh * 64
                vld = vld_pair
                mi = None
                for r in range(RG):
                    def mks2(b, rr_, kb_unused):
                        return lambda E: E.matmul(ps_s[b:b + 32, 0:256],
                                                  qpair[b:b + 64, rr_::RG],
                                                  kpair[b:b + 64, kb_unused, rr_::RG],
                                                  start=(rr_ == 0), stop=(rr_ == RG - 1))
                    mi = op("tensor", mks2(hb, r, g % 2),
                            waits=[q_ready, kld, BG("s")] if r == 0 else (),
                            inc="mm" if r == RG - 1 else None)
                s_done = mi
                kp_guard[g % 2] = mi
                q_guard[0] = mi
                sm1 = op("vector", (lambda b: lambda E: E.reduce_max(smax[0:32, 0:1], ps_s[b:b + 32, 0:256],
                                                                     axis=AX.X, negate=True))(hb),
                         waits=[s_done, C("dv")], inc="dv")
                sm2 = op("scalar", (lambda b: lambda E: E.activation(sexp[0:32, :], ps_s[b:b + 32, 0:256],
                                                                     AF.Exp, bias=smax[0:32, 0:1]))(hb),
                         waits=[sm1], inc="ac")
                bguard["s"] = sm2
                sm3 = op("vector", lambda E: E.reduce_sum(smax[0:32, 1:2], sexp[0:32, :], axis=AX.X),
                         waits=[sm2], inc="dv")
                sm4 = op("vector", lambda E: E.reciprocal(smax[0:32, 2:3], smax[0:32, 1:2]),
                         waits=[sm3], inc="dv")
                sm5 = op("vector", lambda E: E.scalar_tensor_tensor(abf[0:32, :], sexp[0:32, :],
                                                                    smax[0:32, 2:3], sexp[0:32, :],
                                                                    ALU.mult, ALU.bypass),
                         waits=[sm4], inc="dv")
                at_done = None
                for xt in range(2):
                    def mkat(x):
                        return lambda E: E.transpose(ps_t[0:128, x * 32:(x + 1) * 32],
                                                     abf[0:32, x * 128:(x + 1) * 128],
                                                     ident[0:32, 0:32])
                    m = op("tensor", mkat(xt), waits=[sm5, BG("t")], inc="mm")
                    def mkac(x):
                        return lambda E: E.tensor_copy(atr[:, x, :], ps_t[0:128, x * 32:(x + 1) * 32])
                    at_done = op("vector", mkac(xt), waits=[m], inc="dv")
                    bguard["t"] = at_done
                oev = []
                for c in range(4):
                    mi = None
                    for xt in range(2):
                        def mko(x, cc, hh_):
                            return lambda E: E.matmul(ps_y[0:32, cc % 2, cc // 2, :],
                                                      atr[:, x, :],
                                                      vgrp[:, x, cc * 8:(cc + 1) * 8,
                                                           hh_ * HD:(hh_ + 1) * HD],
                                                      start=(x == 0), stop=(x == 1))
                        mi = op("tensor", mko(xt, c, hh),
                                waits=[at_done, vld, BG("y%d%d" % (c % 2, c // 2)), osb_guard[0]]
                                if xt == 0 else (),
                                inc="mm" if xt == 1 else None)
                    vg_guard[0] = mi
                    def mkoe(cc):
                        return lambda E: E.activation(osb[0:32, cc * 512:(cc + 1) * 512],
                                                      ps_y[0:32, cc % 2, cc // 2, :], AF.Copy)
                    ev_o = op("scalar", mkoe(c), waits=[mi], inc="ac")
                    bguard["y%d%d" % (c % 2, c // 2)] = ev_o
                    oev.append(ev_o)
                osp = dma("scalar", "sot",
                          s_otok[:, head * HD:(head + 1) * HD].rearrange("(w r) h -> w r h", r=RG),
                          osb[0:32, :].rearrange("w (r h) -> w r h", r=RG),
                          waits=oev)
                osb_guard[0] = osp
                if tb == 0 and g == 0 and hh == 0:
                    if "q0" in dbg_out:
                        dma("gpsimd", "sdbg", dbg_out["q0"], qpair[:], waits=[q_ready])
                    if "wq0" in dbg_out:
                        dma("gpsimd", "sdbg", dbg_out["wq0"], wqTq[:, 0, :, :], waits=[q_ready])
                    if "xtb0" in dbg_out:
                        dma("gpsimd", "sdbg", dbg_out["xtb0"], xn_tb[:], waits=[q_ready])
                    if "a0" in dbg_out:
                        dma("gpsimd", "sdbg", dbg_out["a0"], abf[0:64, :], waits=[sm5])
                    if "sexp0" in dbg_out:
                        dma("gpsimd", "sdbg", dbg_out["sexp0"], sexp[0:64, :], waits=[sm5])
                    if "at0" in dbg_out:
                        dma("gpsimd", "sdbg", dbg_out["at0"], atr[:], waits=[at_done])
                    if "o0" in dbg_out:
                        dma("gpsimd", "sdbg", dbg_out["o0"], osb[0:32, :], waits=oev)
        heads_done = C("sot")

        ofm_done = None
        sot_base = cnt["sot"] - 16 * 16
        for k in range(NKT):
            ofm_done = dmaT("sync", "sof", ofm[:, k, :], s_otok[:, k * 128:(k + 1) * 128],
                            waits=[("sot", sot_base + 16 * (2 * k + 2)), ofm_guard[0], ofm_done])

        for half in range(2):
            lw = dma("scalar", "swp", wpT[:],
                     s_wpT[:, half * 512:(half + 1) * 512].rearrange("(k p) d -> p k d", p=128),
                     waits=wpT_done + [wpT_guard[0]])
            for tt in range(NT):
                xld = dma("scalar", "sxt%d" % (tt % 2), xtile[:, tt % 2, :],
                          xb[t0 + tt * 128: t0 + (tt + 1) * 128, half * 512:(half + 1) * 512],
                          waits=[xt_guard[tt % 2]])
                def mkpb(t, hf):
                    return lambda E: E.matmul(ps_y[:, t % 2, hf, :],
                                              wT[0:NE, t * 128:(t + 1) * 128],
                                              bdmat[0:NE, hf * 512:(hf + 1) * 512],
                                              start=True, stop=False)
                op("tensor", mkpb(tt, half), waits=[wT_done, bdm_done, ofm_done, lw,
                                                    BG("y%d%d" % (tt % 2, half))])
                mi = None
                for k in range(NKT):
                    def mkp(t, kk, hf):
                        return lambda E: E.matmul(ps_y[:, t % 2, hf, :],
                                                  ofm[:, kk, t * 128:(t + 1) * 128],
                                                  wpT[:, kk, :], start=False, stop=(kk == NKT - 1))
                    mi = op("tensor", mkp(tt, k, half), inc="mm" if k == NKT - 1 else None)
                ofm_guard[0] = mi
                wpT_guard[0] = mi
                def mkcv(t, hf):
                    return lambda E: E.tensor_tensor(combined[:, t, hf * 512:(hf + 1) * 512],
                                                     ps_y[:, t % 2, hf, :],
                                                     xtile[:, t % 2, :], ALU.add)
                cv = op("vector", mkcv(tt, half), waits=[mi, xld, comb_store[0]], inc="dv")
                bguard["y%d%d" % (tt % 2, half)] = cv
                xt_guard[tt % 2] = cv
        comb_init = C("dv")
        if "combe" in dbg_out and tb == 0:
            dma("sync", "sq", dbg_out["combe"][0], combined[:], waits=[comb_init])
        if "comb1" in dbg_out:
            dma("sync", "sq", dbg_out["comb1"][tb], combined[:], waits=[comb_init])
        if "otok" in dbg_out:
            for i in range(TQ // 128):
                ldb = dma("sync", "sq", vstage[:, 0, :], s_otok[i * 128:(i + 1) * 128, :],
                          waits=[ofm_done, C("dv"), C("ac")])
                cv2 = op("vector", lambda E: E.tensor_copy(xsq, vstage[:, 0, :]), waits=[ldb], inc="dv")
                dma("sync", "sq", dbg_out["otok"][tb][i * 128:(i + 1) * 128, :], xsq, waits=[cv2])

        for e in range(NE):
            if tb == 0:
                pump_prep(1)
            epar = e % 2
            bul = bu_done
            for grp in range(NGRP):
                gpar = (e * NGRP + grp) % 2
                wul = dma("scalar", "swu%d" % gpar, wug[:, gpar, :],
                          s_wuT[e][:, grp * G_F * NKT * 128:(grp + 1) * G_F * NKT * 128],
                          waits=wu_ready[e] + [wug_guard[gpar]])
                wdl = dma("scalar", "swd%d" % gpar, wdg[:, gpar, :, :],
                          s_wdT[e][grp * G_F * 128:(grp + 1) * G_F * 128, :]
                          .rearrange("(j p) d -> p j d", p=128),
                          waits=wd_ready[e] + [wdg_guard[gpar]])
                for tc in range(2):
                    up_done = []
                    for j in range(G_F):
                        jpar = j % 2
                        mi = None
                        for k in range(NKT):
                            def mku(jp, jj, kk, gp, cc):
                                return lambda E: E.matmul(
                                    ps_a[:, jp, :],
                                    wug[:, gp, (jj * NKT + kk) * 128:(jj * NKT + kk + 1) * 128],
                                    xn_tb[:, kk, cc * 512:(cc + 1) * 512],
                                    start=(kk == 0), stop=(kk == NKT - 1))
                            mi = op("tensor", mku(jpar, j, k, gpar, tc),
                                    waits=[wul, ld_xn, BG("a%d" % jpar), a1_guard[j]] if k == 0 else (),
                                    inc="mm" if k == NKT - 1 else None)
                        def mkgl(jp, jj, ee, gg):
                            return lambda E: E.activation(
                                a1g[:, jj, :], ps_a[:, jp, :], AF.Gelu,
                                bias=bug9[:, ee, gg * G_F + jj: gg * G_F + jj + 1])
                        gl = op("scalar", mkgl(jpar, j, e, grp), waits=[mi, bul], inc="ac")
                        bguard["a%d" % jpar] = gl
                        up_done.append(gl)
                    if tb == 0 and e == 0 and grp == 0 and tc == 0:
                        if "a10" in dbg_out:
                            dg = dma("gpsimd", "sdbg", dbg_out["a10"], a1g[:], waits=up_done)
                            a1_guard[0] = dg
                        if "wug0" in dbg_out:
                            dg2 = dma("gpsimd", "sdbg", dbg_out["wug0"], wug[:, gpar, :], waits=[wul] + up_done)
                            wug_guard[gpar] = dg2
                    for ts in range(4):
                        tile = tc * 4 + ts
                        for half in range(2):
                            mi = None
                            for j in range(G_F):
                                def mkd(jj, t_, hf, gp):
                                    return lambda E: E.matmul(
                                        ps_y[:, t_ % 2, hf, :],
                                        a1g[:, jj, t_ * 128:(t_ + 1) * 128],
                                        wdg[:, gp, jj, hf * 512:(hf + 1) * 512],
                                        start=(jj == 0), stop=(jj == G_F - 1))
                                mi = op("tensor", mkd(j, ts, half, gpar),
                                        waits=(up_done + [wdl, BG("y%d%d" % (ts % 2, half))])
                                        if j == 0 else (),
                                        inc="mm" if j == G_F - 1 else None)
                            def mkev(t_, hf, ee):
                                return lambda E: E.scalar_tensor_tensor(
                                    combined[:, t_, hf * 512:(hf + 1) * 512],
                                    ps_y[:, t_ % 2, hf, :],
                                    wgate[:, t_, ee:ee + 1],
                                    combined[:, t_, hf * 512:(hf + 1) * 512],
                                    ALU.mult, ALU.add)
                            bguard["y%d%d" % (ts % 2, half)] = op("vector", mkev(tile, half, e),
                               waits=[mi, comb_init, wgate_done], inc="dv")
                    for j in range(G_F):
                        a1_guard[j] = C("mm")
                wug_guard[gpar] = C("mm")
                wdg_guard[gpar] = C("mm")
            if "combe" in dbg_out and tb == 0:
                dma("sync", "sq", dbg_out["combe"][e + 1], combined[:], waits=[C("dv")])
        pb_done = C("dv")

        st = dma("sync", "syo", yout[t0:t0 + TQ, :].rearrange("(t p) d -> p t d", p=128),
                 combined[:], waits=[pb_done])
        comb_store[0] = st
        xn_tb_guard[0] = C("mm")

    if "wut0" in dbg_out:
        stg = xnout[:, 0, :]
        d2 = None
        for ch in range(FF * NKT // D):
            ldb = dma("sync", "sq", stg, s_wuT[3][:, ch * D:(ch + 1) * D],
                      waits=wu_ready[3] + [d2, C("dv")] if wu_ready[0] else [d2, C("dv")])
            cv = op("vector", lambda E: E.tensor_copy(xcent, stg), waits=[ldb], inc="dv")
            d2 = dma("sync", "sq", dbg_out["wut0"][:, ch * D:(ch + 1) * D], xcent, waits=[cv])
    if "wdt0" in dbg_out:
        d2 = None
        for ch in range(FF // 128):
            ldb = dma("sync", "sq", xnout[:, 0, :], s_wdT[3][ch * 128:(ch + 1) * 128, :],
                      waits=wd_ready[3] + [d2, C("dv")] if wd_ready[0] else [d2, C("dv")])
            cv = op("vector", lambda E: E.tensor_copy(xcent, xnout[:, 0, :]), waits=[ldb], inc="dv")
            d2 = dma("sync", "sq", dbg_out["wdt0"][ch * 128:(ch + 1) * 128, :], xcent, waits=[cv])

    op("sync", lambda E: E.nop(), waits=[C(s) for s in SEMNAMES])

    nc.all_engine_barrier()
    semnums = sorted(sems[s].num for s in SEMNAMES)
    assert semnums == list(range(semnums[0], semnums[0] + len(semnums))), semnums
    clr_range = range(semnums[0], semnums[-1] + 1)
    nc.gpsimd.dma_reset(clr_range)
    nc.gpsimd.sem_clear(clr_range)
    nc.all_engine_barrier()
    with nc.Block() as block:
        def mkrunner(eng_name):
            def runner(E):
                for waits, fn, inc, k in SCHED[eng_name]:
                    for sname, val in waits:
                        E.wait_ge(sems[sname], val)
                    ins = fn(E)
                    if inc is not None:
                        ins.then_inc(sems[inc], k)
            return runner
        block.tensor(mkrunner("tensor"))
        block.vector(mkrunner("vector"))
        block.scalar(mkrunner("scalar"))
        block.gpsimd(mkrunner("gpsimd"))
        block.sync(mkrunner("sync"))
    es.close()
    return nc, dbg_out


import numpy as np
from concourse.bass_utils import run_bass_kernel_spmd

_NC_CACHE = {}

def _get_nc():
    if "nc" not in _NC_CACHE:
        _NC_CACHE["nc"] = build(dbg=())[0]
    return _NC_CACHE["nc"]


def kernel(L0_attn_x, L0_attn_Wqkv, L0_attn_Wproj, L0_ffn_gate_w, L0_ffn_gate_bias,
           L0_ffn_Wu, L0_ffn_bu, L0_ffn_Wd, L0_ffn_bd, L0_ffn_sWu, L0_ffn_sbu,
           L0_ffn_sWd, L0_ffn_sbd):
    x = np.ascontiguousarray(np.asarray(L0_attn_x, dtype=np.float32))
    com = dict(
        wqkv=np.ascontiguousarray(np.asarray(L0_attn_Wqkv, np.float32)),
        wproj=np.ascontiguousarray(np.asarray(L0_attn_Wproj, np.float32)),
        gw=np.ascontiguousarray(np.asarray(L0_ffn_gate_w, np.float32)),
        gb=np.ascontiguousarray(np.asarray(L0_ffn_gate_bias, np.float32).reshape(1, 8)),
        wu=np.ascontiguousarray(np.asarray(L0_ffn_Wu, np.float32)),
        bu=np.ascontiguousarray(np.asarray(L0_ffn_bu, np.float32)),
        wd=np.ascontiguousarray(np.asarray(L0_ffn_Wd, np.float32)),
        bd=np.ascontiguousarray(np.asarray(L0_ffn_bd, np.float32)),
        swu=np.ascontiguousarray(np.asarray(L0_ffn_sWu, np.float32)),
        sbu=np.ascontiguousarray(np.asarray(L0_ffn_sbu, np.float32).reshape(1, FF)),
        swd=np.ascontiguousarray(np.asarray(L0_ffn_sWd, np.float32)),
        sbd=np.ascontiguousarray(np.asarray(L0_ffn_sbd, np.float32).reshape(1, D)),
    )
    in_maps = []
    for c in range(8):
        b, half = c // 2, c % 2
        xb_ = x[b]
        if half == 1:
            xb_ = np.concatenate([xb_[TLOC:], xb_[:TLOC]], axis=0)
        m = dict(com)
        m["xb"] = np.ascontiguousarray(xb_)
        in_maps.append(m)
    nc = _get_nc()
    res = run_bass_kernel_spmd(nc, in_maps, core_ids=list(range(8)))
    out = np.empty((B, NSEQ, D), dtype=np.float32)
    for c in range(8):
        b, half = c // 2, c % 2
        out[b, half * TLOC:(half + 1) * TLOC, :] = res.results[c]["yout"]
    return out


# version 08
# speedup vs baseline: 1.0540x; 1.0540x over previous
import contextlib
import concourse.bass as bass
import concourse.mybir as mybir

F32 = mybir.dt.float32
BF16 = mybir.dt.bfloat16
AF = mybir.ActivationFunctionType
ALU = mybir.AluOpType
AX = mybir.AxisListType

B, NSEQ, D = 4, 8192, 1024
H, HD = 16, 64
RG, XG = 32, 256
FF = 4096
NE = 9
NTB, TQ = 4, 1024
TLOC = NTB * TQ
NKT = D // 128
NT = TQ // 128
G_F = 4
NGRP = FF // 128 // G_F
NHP = H // 2

ENGS = ("tensor", "vector", "scalar", "gpsimd", "sync")


def build(dbg=()):
    nc = bass.Bass("TRN2", target_bir_lowering=False, debug=False)

    xb = nc.dram_tensor("xb", [NSEQ, D], F32, kind="ExternalInput").ap()
    wqkv = nc.dram_tensor("wqkv", [3 * D, D], F32, kind="ExternalInput").ap()
    wproj = nc.dram_tensor("wproj", [D, D], F32, kind="ExternalInput").ap()
    gw = nc.dram_tensor("gw", [8, D], F32, kind="ExternalInput").ap()
    gb = nc.dram_tensor("gb", [1, 8], F32, kind="ExternalInput").ap()
    wu_in = nc.dram_tensor("wu", [8, FF, D], F32, kind="ExternalInput").ap()
    bu_in = nc.dram_tensor("bu", [8, FF], F32, kind="ExternalInput").ap()
    wd_in = nc.dram_tensor("wd", [8, D, FF], F32, kind="ExternalInput").ap()
    bd_in = nc.dram_tensor("bd", [8, D], F32, kind="ExternalInput").ap()
    swu_in = nc.dram_tensor("swu", [FF, D], F32, kind="ExternalInput").ap()
    sbu_in = nc.dram_tensor("sbu", [1, FF], F32, kind="ExternalInput").ap()
    swd_in = nc.dram_tensor("swd", [D, FF], F32, kind="ExternalInput").ap()
    sbd_in = nc.dram_tensor("sbd", [1, D], F32, kind="ExternalInput").ap()

    yout = nc.dram_tensor("yout", [TLOC, D], F32, kind="ExternalOutput").ap()

    s_wq_bf = nc.dram_tensor("s_wq_bf", [3 * D, D], BF16, kind="Internal").ap()
    s_wp_bf = nc.dram_tensor("s_wp_bf", [D, D], BF16, kind="Internal").ap()
    s_wu_bf = nc.dram_tensor("s_wu_bf", [FF, D], BF16, kind="Internal").ap()
    s_wd_bf = nc.dram_tensor("s_wd_bf", [D, FF], BF16, kind="Internal").ap()
    s_wu_bf2 = nc.dram_tensor("s_wu_bf2", [2, NKT, FF, 128], BF16, kind="Internal").ap()
    s_wd_bf2 = nc.dram_tensor("s_wd_bf2", [2, FF // 128, D, 128], BF16, kind="Internal").ap()
    s_wqT = nc.dram_tensor("s_wqT", [D, 3 * D], BF16, kind="Internal").ap()
    s_wpT = nc.dram_tensor("s_wpT", [D, D], BF16, kind="Internal").ap()
    s_wuT = nc.dram_tensor("s_wuT", [NE, 128, FF * NKT], BF16, kind="Internal").ap()
    s_wdT = nc.dram_tensor("s_wdT", [NE, FF, D], BF16, kind="Internal").ap()
    s_xn_tok = nc.dram_tensor("s_xn_tok", [NSEQ, D], BF16, kind="Internal").ap()
    s_xn_fm = nc.dram_tensor("s_xn_fm", [D, NSEQ], BF16, kind="Internal").ap()
    s_kfm = nc.dram_tensor("s_kfm", [NHP, 128, NSEQ], BF16, kind="Internal").ap()
    s_vtok = nc.dram_tensor("s_vtok", [NSEQ, D], BF16, kind="Internal").ap()
    s_otok = nc.dram_tensor("s_otok", [TQ, D], BF16, kind="Internal").ap()

    dbg_out = {}
    def _dbg(name, shape):
        if name in dbg:
            dbg_out[name] = nc.dram_tensor("d_" + name, shape, F32, kind="ExternalOutput").ap()
    _dbg("xn", [NSEQ, D])
    _dbg("xnfm", [D, NSEQ])
    _dbg("wqt", [D, 3 * D])
    _dbg("kfm", [NHP, 128, NSEQ])
    _dbg("vtok", [NSEQ, D])
    _dbg("otok", [NTB, TQ, D])
    _dbg("w", [NTB, 128, NT, NE])
    _dbg("q0", [128, TQ])
    _dbg("wut0", [128, FF * NKT])
    _dbg("combe", [NE + 1, 128, NT, D])
    _dbg("a10", [128, G_F, 512])
    _dbg("wug0", [128, G_F * NKT * 128])
    _dbg("wdt0", [FF, D])
    _dbg("wq0", [128, NKT, 128])
    _dbg("xtb0", [128, NKT, TQ])
    _dbg("a0", [64, XG])
    _dbg("at0", [128, 2, RG])
    _dbg("o0", [32, RG * HD])
    _dbg("sexp0", [64, XG])
    _dbg("comb1", [NTB, 128, NT, D])

    es = contextlib.ExitStack()
    def sbt(name, shape, dt):
        return es.enter_context(nc.sbuf_tensor(name, shape, dt))
    def pst(name, shape, dt):
        return es.enter_context(nc.psum_tensor(name, shape, dt))

    combined = sbt("combined", [128, NT, D], F32)
    xn_tb = sbt("xn_tb", [128, NKT, TQ], BF16)
    tstage = sbt("tstage", [128, 2, 2048], BF16)
    kpair = sbt("kpair", [128, 2, NSEQ], BF16)
    qpair = sbt("qpair", [128, TQ], BF16)
    vgrp = sbt("vgrp", [128, 2, RG, 128], BF16)
    sexp = sbt("sexp", [64, XG], F32)
    smax = sbt("smax", [64, 4], F32)
    abf = sbt("abf", [64, XG], BF16)
    atr = sbt("atr", [128, 2, RG], BF16)
    osb = sbt("osb", [32, RG * HD], BF16)
    ofm = sbt("ofm", [128, NKT, TQ], BF16)
    wpT = sbt("wpT", [128, NKT, 512], BF16)
    xnstr = sbt("xnstr", [128, 2, NKT, 256], BF16)
    kvst = sbt("kvst", [128, 2, 512], BF16)
    vstage = sbt("vstage", [128, 2, D], BF16)
    wqTq = sbt("wqTq", [128, 2, NKT, 128], BF16)
    scst = sbt("scst", [128, NT, 8], F32)
    gtmp = sbt("gtmp", [128, NT, 8], F32)
    gtmp2 = sbt("gtmp2", [128, NT, 8], F32)
    gred = sbt("gred", [128, NT, 4], F32)
    wgate = sbt("wgate", [128, NT, NE], F32)
    wgate_bf = sbt("wgate_bf", [128, NT, NE], BF16)
    wT = sbt("wT", [NE, TQ], BF16)
    bdmat = sbt("bdmat", [NE, D], BF16)
    gbias = sbt("gbias", [1, 8], BF16)
    gw_bf = sbt("gw_bf", [8, D], BF16)
    gwT = sbt("gwT", [128, NKT, 8], BF16)
    ident = sbt("ident", [128, 128], BF16)
    ones_bf = sbt("ones_bf", [1, 128], BF16)
    stat = sbt("stat", [128, 8], F32)
    wug = sbt("wug", [128, 2, G_F * NKT * 128], BF16)
    wdg = sbt("wdg", [128, 2, G_F, D], BF16)
    xch = combined[:, 0:2, :]
    xcent = combined[:, 2, :]
    xsq = combined[:, 3, :]
    xnout = combined[:, 4:5, :].bitcast(BF16).rearrange("p o (a b) -> p (o a) b", a=2)
    a1g = sbt("a1g", [128, G_F, 512], BF16)
    bug9 = sbt("bug9", [128, NE, FF // 128], F32)
    bustage = sbt("bustage", [32, 128], F32)
    identf = sbt("identf", [128, 128], F32)
    xtile = sbt("xtile", [128, 2, 512], F32)

    wqTk_v = wug[:].rearrange("p b (j k c) -> p (b j) k c", k=NKT, c=128)
    wqTv_v = wdg[:].rearrange("p b j d -> p (b j) d")

    ps_a = pst("ps_a", [128, 2, 512], F32)
    ps_y = pst("ps_y", [128, 2, 2, 512], F32)
    ps_s = pst("ps_s", [128, 512], F32)
    ps_t = pst("ps_t", [128, 1024], BF16)

    SEMNAMES = ("mm", "dv", "ac", "gc", "sq", "aq", "gq",
                "tsti0", "tsti1", "tsto0", "tsto1", "sxp", "scu", "scd", "skv",
                "sot", "sof", "sxn", "swq0", "swq1", "skp", "svg0", "svg1",
                "sxs0", "sxs1", "swu0", "swu1", "swd0", "swd1", "swp",
                "sxt0", "sxt1", "sxc0", "sxc1", "sbu", "smisc", "syo", "sdbg")
    sems = {s: es.enter_context(nc.semaphore(s)) for s in SEMNAMES}
    cnt = {s: 0 for s in sems}
    SCHED = {e: [] for e in ENGS}
    last_wait = {e: {} for e in ENGS}

    def op(eng, fn, waits=(), inc=None, k=1):
        wl = []
        for wv in waits:
            if wv is None:
                continue
            sname, val = wv
            if val <= 0:
                continue
            if val > last_wait[eng].get(sname, -1):
                wl.append((sname, val))
                last_wait[eng][sname] = val
        SCHED[eng].append((tuple(wl), fn, inc, k))
        if inc is not None:
            cnt[inc] += k
            return (inc, cnt[inc])
        return None

    def dma(eng, sem, out_ap, in_ap, waits=(), **kw):
        def mk(o, i, kw2):
            return lambda E: E.dma_start(out=o, in_=i, **kw2)
        return op(eng, mk(out_ap, in_ap, kw), waits=waits, inc=sem, k=16)

    def dmaT(eng, sem, out_ap, in_ap, waits=()):
        def mk(o, i):
            return lambda E: E.dma_start_transpose(o, i)
        return op(eng, mk(out_ap, in_ap), waits=waits, inc=sem, k=16)

    def C(s):
        return (s, cnt[s])

    bguard = {}
    def BG(key):
        return bguard.get(key)

    def CT():
        return [("tsto0", cnt["tsto0"]), ("tsto1", cnt["tsto1"])]

    rr = [0]
    tst_guard = [None, None]
    last_t1 = [None]

    def transpose_stage(src_ap, dst_ap, ncols, waits):
        par = rr[0] % 2
        rr[0] += 1
        w = list(waits) + [tst_guard[par], last_t1[0]]
        t1 = dmaT("sync", "tsti%d" % par, tstage[:, par, 0:ncols], src_ap, waits=w)
        last_t1[0] = t1
        t2 = dma("gpsimd", "tsto%d" % par, dst_ap, tstage[:, par, 0:ncols], waits=[t1])
        tst_guard[par] = t2
        return t2

    i0 = op("gpsimd", lambda E: E.memset(ident[:], 1.0), inc="gc")
    ident_done = op("gpsimd", lambda E: E.affine_select(ident[:], ident[:], [[1, 128]],
                                                        ALU.is_equal, 0.0, base=0,
                                                        channel_multiplier=-1),
                    waits=[i0], inc="gc")
    op("gpsimd", lambda E: E.memset(ones_bf[:], 1.0), inc="gc")
    ones_done = C("gc")
    dma("gpsimd", "smisc", gbias[:], gb)
    dma("gpsimd", "smisc", bdmat[0:8, :], bd_in)
    dma("gpsimd", "smisc", bdmat[8:9, :], sbd_in)
    dma("gpsimd", "smisc", gw_bf[:], gw)
    misc_done = C("smisc")
    bdm_done = misc_done
    cgw = misc_done

    idf = op("vector", lambda E: E.tensor_copy(identf[:], ident[:]), waits=[ident_done], inc="dv")
    bu_done = None
    for e in range(NE):
        _src = (bu_in[e:e + 1, :] if e < 8 else sbu_in).rearrange("o (j p) -> (o j) p", p=128)
        bl = dma("gpsimd", "sbu", bustage[:], _src, waits=[bu_done])
        def mkbt(ee):
            return lambda E: E.transpose(ps_s[0:128, 256:288].bitcast(F32) if False else ps_s[0:128, 256:288],
                                         bustage[:], identf[0:32, 0:32])
        mt = op("tensor", mkbt(e), waits=[bl, idf, BG("s")], inc="mm")
        def mkbc(ee):
            return lambda E: E.tensor_copy(bug9[:, ee, :], ps_s[0:128, 256:288])
        bu_done = op("vector", mkbc(e), waits=[mt], inc="dv")
        bguard["s"] = bu_done
    cw = dma("gpsimd", "scu", s_wq_bf, wqkv)
    for k in range(NKT):
        for hh in range(2):
            transpose_stage(s_wq_bf[hh * 1536:(hh + 1) * 1536, k * 128:(k + 1) * 128],
                            s_wqT[k * 128:(k + 1) * 128, hh * 1536:(hh + 1) * 1536],
                            1536, [cw])
    wqT_done = CT()
    cw = dma("gpsimd", "scu", s_wp_bf, wproj, waits=wqT_done)
    for k in range(NKT):
        for hh in range(2):
            transpose_stage(s_wp_bf[hh * 512:(hh + 1) * 512, k * 128:(k + 1) * 128],
                            s_wpT[k * 128:(k + 1) * 128, hh * 512:(hh + 1) * 512],
                            512, [cw])
    wpT_done = CT()
    for k in range(NKT):
        def mk_gwt(kk):
            return lambda E: E.transpose(ps_t[0:128, 0:8], gw_bf[0:8, kk * 128:(kk + 1) * 128],
                                         ident[0:8, 0:8])
        m = op("tensor", mk_gwt(k), waits=[cgw, ident_done, BG("t")], inc="mm")
        def mk_gwc(kk):
            return lambda E: E.tensor_copy(gwT[:, kk, :], ps_t[0:128, 0:8])
        bguard["t"] = op("vector", mk_gwc(k), waits=[m], inc="dv")
    gwT_done = C("dv")

    wu_ready = [None] * NE
    wd_ready = [None] * NE
    wuT_view = s_wuT.rearrange("e p (t k c) -> e p t k c", k=NKT, c=128)

    def prep_expert(e):
        cu = dma("gpsimd", "scu", s_wu_bf, (wu_in[e] if e < 8 else swu_in),
                 waits=(wu_ready[e - 1] if e > 0 else wpT_done))
        cd = dma("gpsimd", "scd", s_wd_bf, (wd_in[e] if e < 8 else swd_in),
                 waits=(wd_ready[e - 1] if e > 0 else []))
        for k in range(NKT):
            for hh in range(2):
                dst = wuT_view[e][:, hh * 16:(hh + 1) * 16, k, :]
                srca = s_wu_bf[hh * 2048:(hh + 1) * 2048, k * 128:(k + 1) * 128]
                transpose_stage(srca, dst, 2048, [cu])
        wu_ready[e] = CT()
        for f in range(FF // 128):
            transpose_stage(s_wd_bf[:, f * 128:(f + 1) * 128],
                            s_wdT[e][f * 128:(f + 1) * 128, :], D, [cd])
        wd_ready[e] = CT()
    prep_queue = list(range(NE))

    def pump_prep(n=1):
        for _ in range(n):
            if prep_queue:
                prep_expert(prep_queue.pop(0))

    xch_guard = [None, None]
    xn_last = None
    for i in range(NSEQ // 128):
        par = i % 2
        ld = dma("sync", "sxc%d" % par, xch[:, par, :], xb[i * 128:(i + 1) * 128, :],
                 waits=[xch_guard[par]])
        def _r1(p):
            return lambda E: E.reduce_sum(stat[:, 0:1], xch[:, p, :], axis=AX.X)
        r1 = op("vector", _r1(par), waits=[ld, C("dv")], inc="dv")
        r2 = op("vector", lambda E: E.tensor_scalar(stat[:, 1:2], stat[:, 0:1], 1.0 / D, None, ALU.mult),
                waits=[r1], inc="dv")
        def _r3(p):
            return lambda E: E.scalar_tensor_tensor(xcent, xch[:, p, :], stat[:, 1:2],
                                                    xch[:, p, :], ALU.subtract, ALU.bypass)
        r3 = op("vector", _r3(par), waits=[r2], inc="dv")
        xch_guard[par] = r3
        r4 = op("scalar", lambda E: E.activation(xsq, xcent, AF.Square), waits=[r3, C("ac")], inc="ac")
        r5 = op("vector", lambda E: E.reduce_sum(stat[:, 2:3], xsq, axis=AX.X), waits=[r4], inc="dv")
        r6 = op("vector", lambda E: E.tensor_scalar(stat[:, 3:4], stat[:, 2:3], 1.0 / D, 1.0e-5,
                                                    ALU.mult, ALU.add), waits=[r5], inc="dv")
        r7 = op("scalar", lambda E: E.activation(stat[:, 4:5], stat[:, 3:4], AF.Sqrt), waits=[r6], inc="ac")
        r8 = op("vector", lambda E: E.reciprocal(stat[:, 5:6], stat[:, 4:5]), waits=[r7], inc="dv")
        def _r9(p):
            return lambda E: E.scalar_tensor_tensor(xnout[:, p, :], xcent, stat[:, 5:6],
                                                    xcent, ALU.mult, ALU.bypass)
        r9 = op("vector", _r9(par), waits=[r8], inc="dv")
        xn_last = dma("gpsimd", "sxp", s_xn_tok[i * 128:(i + 1) * 128, :], xnout[:, par, :], waits=[r9])
    xn_tok_done = xn_last
    ln_done = [("sxp", 16 * 16 * (ch + 1)) for ch in range(4)]

    if "xn" in dbg_out:
        for i in range(NSEQ // 128):
            ldb = dma("sync", "sq", xnout[:, 0, :], s_xn_tok[i * 128:(i + 1) * 128, :],
                      waits=[xn_tok_done, C("dv")])
            cv = op("vector", lambda E: E.tensor_copy(xcent, xnout[:, 0, :]), waits=[ldb], inc="dv")
            dma("sync", "sq", dbg_out["xn"][i * 128:(i + 1) * 128, :], xcent, waits=[cv])

    xnq_done = []
    for ch in range(4):
        for k in range(NKT):
            transpose_stage(
                s_xn_tok[ch * 2048:(ch + 1) * 2048, k * 128:(k + 1) * 128],
                s_xn_fm[k * 128:(k + 1) * 128, ch * 2048:(ch + 1) * 2048],
                2048, [ln_done[ch]])
        xnq_done.append(CT())
    xnfm_done = CT()

    pump_prep(2)
    if "xnfm" in dbg_out:
        for i in range(D // 128):
            ldb = dma("sync", "sq", xnout[:, 0, :].bitcast(BF16) if False else xnout[:, 0, :],
                      s_xn_fm[i * 128:(i + 1) * 128, 0:D], waits=xnfm_done + [C("dv")])
            cv = op("vector", lambda E: E.tensor_copy(xcent, xnout[:, 0, :]), waits=[ldb], inc="dv")
            d2 = dma("sync", "sq", dbg_out["xnfm"][i * 128:(i + 1) * 128, 0:D], xcent, waits=[cv])
            for ch in range(1, NSEQ // D):
                ldb = dma("sync", "sq", xnout[:, 0, :], s_xn_fm[i * 128:(i + 1) * 128, ch * D:(ch + 1) * D],
                          waits=[d2, C("dv")])
                cv = op("vector", lambda E: E.tensor_copy(xcent, xnout[:, 0, :]), waits=[ldb], inc="dv")
                d2 = dma("sync", "sq", dbg_out["xnfm"][i * 128:(i + 1) * 128, ch * D:(ch + 1) * D],
                         xcent, waits=[cv])
    if "wqt" in dbg_out:
        for i in range(D // 128):
            d2 = None
            for ch in range(3):
                ldb = dma("sync", "sq", xnout[:, 0, :], s_wqT[i * 128:(i + 1) * 128, ch * D:(ch + 1) * D],
                          waits=wqT_done + [d2, C("dv")])
                cv = op("vector", lambda E: E.tensor_copy(xcent, xnout[:, 0, :]), waits=[ldb], inc="dv")
                d2 = dma("sync", "sq", dbg_out["wqt"][i * 128:(i + 1) * 128, ch * D:(ch + 1) * D],
                         xcent, waits=[cv])
    wug_flat = wug[:].rearrange("p b f -> p (b f)")
    l_k = None
    for g_ in range(NHP):
        l_k = dma("scalar", "swu0",
                  wug_flat[:, g_ * (NKT * 128):(g_ + 1) * (NKT * 128)]
                  .rearrange("p (k c) -> p k c", c=128),
                  s_wqT[:, D + g_ * 128:D + (g_ + 1) * 128].rearrange("(k p) c -> p k c", p=128),
                  waits=wqT_done)
    l_v = dma("scalar", "swd0", wqTv_v,
              s_wqT[:, 2 * D:3 * D].rearrange("(k p) v -> p k v", p=128),
              waits=wqT_done)

    xnstr_guard = [None, None]
    kvev_guard = [None, None]
    vst_guard = [None, None]
    wug_guard = [None, None]
    wdg_guard = [None, None]
    for tcx in range(NSEQ // 256):
        par = tcx % 2
        ldx = dma("scalar", "sxs%d" % par, xnstr[:, par, :, :],
                  s_xn_fm[:, tcx * 256:(tcx + 1) * 256].rearrange("(k p) t -> p k t", p=128),
                  waits=xnq_done[tcx // 8] + [xnstr_guard[par]])
        last_consumer = None
        for g in range(NHP):
            mi = None
            for k in range(NKT):
                def mk(gg, kk, p):
                    return lambda E: E.matmul(ps_a[:, gg % 2, 0:256], wqTk_v[:, gg, kk, :],
                                              xnstr[:, p, kk, :],
                                              start=(kk == 0), stop=(kk == NKT - 1))
                mi = op("tensor", mk(g, k, par),
                        waits=[ldx, l_k, BG("a%d" % (g % 2))] if k == 0 else (),
                        inc="mm" if k == NKT - 1 else None)
            def mkev(gg):
                return lambda E: E.activation(kvst[:, gg % 2, 0:256], ps_a[:, gg % 2, 0:256], AF.Copy)
            ev = op("scalar", mkev(g), waits=[mi, kvev_guard[g % 2]], inc="ac")
            bguard["a%d" % (g % 2)] = ev
            sp = dma("scalar", "skv", s_kfm[g][:, tcx * 256:(tcx + 1) * 256], kvst[:, g % 2, 0:256],
                     waits=[ev])
            kvev_guard[g % 2] = sp
            last_consumer = mi
        for ts in range(2):
            evs = []
            for half in range(2):
                mi = None
                for k in range(NKT):
                    def mkv(hh, kk, p, t):
                        return lambda E: E.matmul(ps_y[:, t, hh, :],
                                                  xnstr[:, p, kk, t * 128:(t + 1) * 128],
                                                  wqTv_v[:, kk, hh * 512:(hh + 1) * 512],
                                                  start=(kk == 0), stop=(kk == NKT - 1))
                    mi = op("tensor", mkv(half, k, par, ts),
                            waits=[ldx, l_v, BG("y%d0" % ts), BG("y%d1" % ts)]
                            if (k == 0 and half == 0) else (),
                            inc="mm" if k == NKT - 1 else None)
                def mkve(hh, t):
                    return lambda E: E.activation(vstage[:, t, hh * 512:(hh + 1) * 512],
                                                  ps_y[:, t, hh, :], AF.Copy)
                ev_v = op("scalar", mkve(half, ts), waits=[mi, vst_guard[ts]], inc="ac")
                bguard["y%d%d" % (ts, half)] = ev_v
                evs.append(ev_v)
            sp = dma("scalar", "skv", s_vtok[tcx * 256 + ts * 128: tcx * 256 + (ts + 1) * 128, :],
                     vstage[:, ts, :], waits=evs)
            vst_guard[ts] = sp
            last_consumer = mi
        xnstr_guard[par] = last_consumer
    kv_done_all = [C("skv")]
    wug_guard[0] = wug_guard[1] = C("mm")
    wdg_guard[0] = wdg_guard[1] = C("mm")

    if "kfm" in dbg_out:
        for g in range(NHP):
            for ch in range(NSEQ // 512):
                ldb = dma("sync", "sq", kvst[:, 0, :], s_kfm[g][:, ch * 512:(ch + 1) * 512],
                          waits=kv_done_all + [C("dv")])
                cv = op("vector", lambda E: E.tensor_copy(xsq[:, 0:512], kvst[:, 0, :]),
                        waits=[ldb], inc="dv")
                dma("sync", "sq", dbg_out["kfm"][g][:, ch * 512:(ch + 1) * 512], xsq[:, 0:512],
                    waits=[cv])
    if "vtok" in dbg_out:
        for i in range(NSEQ // 128):
            ldb = dma("sync", "sq", vstage[:, 0, :], s_vtok[i * 128:(i + 1) * 128, :],
                      waits=kv_done_all + [C("dv"), C("ac")])
            cv = op("vector", lambda E: E.tensor_copy(xsq, vstage[:, 0, :]), waits=[ldb], inc="dv")
            dma("sync", "sq", dbg_out["vtok"][i * 128:(i + 1) * 128, :], xsq, waits=[cv])

    xn_tb_guard = [None]
    comb_store = [None]
    kp_guard = [None, None]
    q_guard = [None]
    vg_guard = [None, None]
    wq_guard = [None, None]
    osb_guard = [None]
    ofm_guard = [None]
    wpT_guard = [None]
    xt_guard = [None, None]
    a1_guard = [None] * G_F

    for tb in range(NTB):
        t0 = tb * TQ

        ld_xn = dma("scalar", "sxn", xn_tb[:],
                    s_xn_fm[:, t0:t0 + TQ].rearrange("(k p) t -> p k t", p=128),
                    waits=xnq_done[tb // 2] + [xn_tb_guard[0]])

        for tt in range(NT):
            def mkb(t):
                return lambda E: E.matmul(ps_a[:, t % 2, 0:8], ones_bf[0:1, 0:128], gbias[0:1, :],
                                          start=True, stop=False)
            op("tensor", mkb(tt), waits=[ld_xn, gwT_done, ones_done, BG("a%d" % (tt % 2))])
            mi = None
            for k in range(NKT):
                def mkg(t, kk):
                    return lambda E: E.matmul(ps_a[:, t % 2, 0:8], xn_tb[:, kk, t * 128:(t + 1) * 128],
                                              gwT[:, kk, :], start=False, stop=(kk == NKT - 1))
                mi = op("tensor", mkg(tt, k), inc="mm" if k == NKT - 1 else None)
            def mks(t):
                return lambda E: E.activation(scst[:, t, :], ps_a[:, t % 2, 0:8], AF.Sigmoid)
            bguard["a%d" % (tt % 2)] = op("scalar", mks(tt), waits=[mi], inc="ac")
        gate_sc = C("ac")

        def bget(apc):
            return apc.rearrange("p (g o) -> p g o", o=1).broadcast_to([128, NT, 8])
        g1 = op("vector", lambda E: E.reduce_max(gred[:, :, 0], scst[:], axis=AX.X),
                waits=[gate_sc, C("dv")], inc="dv")
        g2 = op("vector", lambda E: E.tensor_tensor(gtmp[:], scst[:], bget(gred[:, :, 0]), ALU.is_equal),
                waits=[g1], inc="dv")
        g3 = op("vector", lambda E: E.scalar_tensor_tensor(gtmp2[:], gtmp[:], -1.0e30, scst[:],
                                                           ALU.mult, ALU.add), waits=[g2], inc="dv")
        g4 = op("vector", lambda E: E.reduce_max(gred[:, :, 1], gtmp2[:], axis=AX.X), waits=[g3], inc="dv")
        g5 = op("vector", lambda E: E.tensor_tensor(gtmp[:], scst[:], bget(gred[:, :, 1]), ALU.is_ge),
                waits=[g4], inc="dv")
        g6 = op("vector", lambda E: E.tensor_tensor(gred[:, :, 2], gred[:, :, 0], gred[:, :, 1], ALU.add),
                waits=[g5], inc="dv")
        g7 = op("vector", lambda E: E.tensor_scalar(gred[:, :, 2], gred[:, :, 2], 1.0e-6, None, ALU.add),
                waits=[g6], inc="dv")
        g8 = op("vector", lambda E: E.reciprocal(gred[:, :, 3], gred[:, :, 2]), waits=[g7], inc="dv")
        g9 = op("vector", lambda E: E.tensor_tensor(gtmp2[:], scst[:], gtmp[:], ALU.mult), waits=[g8], inc="dv")
        g10 = op("vector", lambda E: E.tensor_tensor(wgate[:, :, 0:8], gtmp2[:], bget(gred[:, :, 3]), ALU.mult),
                 waits=[g9], inc="dv")
        g11 = op("vector", lambda E: E.memset(wgate[:, :, 8:9], 0.1), waits=[g10], inc="dv")
        g12 = op("vector", lambda E: E.tensor_copy(wgate_bf[:], wgate[:]), waits=[g11], inc="dv")
        wgate_done = g11
        if "w" in dbg_out:
            dma("sync", "sq", dbg_out["w"][tb], wgate[:], waits=[wgate_done])
        wT_last = None
        for tt in range(NT):
            def mkt(t):
                return lambda E: E.transpose(ps_t[0:NE, 128:256], wgate_bf[:, t, :], ident[:])
            m = op("tensor", mkt(tt), waits=[g12, BG("t"), ident_done], inc="mm")
            def mkc(t):
                return lambda E: E.tensor_copy(wT[0:NE, t * 128:(t + 1) * 128], ps_t[0:NE, 128:256])
            wT_last = op("vector", mkc(tt), waits=[m], inc="dv")
            bguard["t"] = wT_last
        wT_done = wT_last

        for g in range(NHP):
            kld = dma("scalar", "skp", kpair[:, g % 2, :], s_kfm[g],
                      waits=kv_done_all + [kp_guard[g % 2]])
            qw = dma("scalar", "swq%d" % (g % 2), wqTq[:, g % 2, :, :],
                     s_wqT[:, g * 128:(g + 1) * 128].rearrange("(k p) c -> p k c", p=128),
                     waits=wqT_done + [wq_guard[g % 2]])
            q_ready = None
            for ch in range(TQ // 512):
                mi = None
                for k in range(NKT):
                    def mkq(gg, kk, cc):
                        return lambda E: E.matmul(ps_a[:, cc % 2, :],
                                                  wqTq[:, gg % 2, kk, :],
                                                  xn_tb[:, kk, cc * 512:(cc + 1) * 512],
                                                  start=(kk == 0), stop=(kk == NKT - 1))
                    mi = op("tensor", mkq(g, k, ch),
                            waits=[qw, ld_xn, q_guard[0], BG("a%d" % (ch % 2))] if k == 0 else (),
                            inc="mm" if k == NKT - 1 else None)
                wq_guard[g % 2] = mi
                def mkqe(cc):
                    return lambda E: E.activation(qpair[:, cc * 512:(cc + 1) * 512], ps_a[:, cc % 2, :],
                                                  AF.Copy, scale=0.125)
                q_ready = op("scalar", mkqe(ch), waits=[mi], inc="ac")
                bguard["a%d" % (ch % 2)] = q_ready

            if tb == 0 and g in (2, 4, 6):
                pump_prep(1)
            vld_pair = None
            for xt in range(2):
                vld_pair = dma("scalar", "svg0",
                               vgrp[:, xt, :, :],
                               s_vtok[xt * 4096:(xt + 1) * 4096, g * 128:(g + 1) * 128]
                               .rearrange("(x r) c -> x r c", r=RG),
                               waits=(kv_done_all + [vg_guard[0]]) if xt == 0 else ())
            for hh in range(2):
                head = 2 * g + hh
                hb = hh * 64
                vld = vld_pair
                mi = None
                for r in range(RG):
                    def mks2(b, rr_, kb_unused):
                        return lambda E: E.matmul(ps_s[b:b + 32, 0:256],
                                                  qpair[b:b + 64, rr_::RG],
                                                  kpair[b:b + 64, kb_unused, rr_::RG],
                                                  start=(rr_ == 0), stop=(rr_ == RG - 1))
                    mi = op("tensor", mks2(hb, r, g % 2),
                            waits=[q_ready, kld, BG("s")] if r == 0 else (),
                            inc="mm" if r == RG - 1 else None)
                s_done = mi
                kp_guard[g % 2] = mi
                q_guard[0] = mi
                sm1 = op("vector", (lambda b: lambda E: E.reduce_max(smax[0:32, 0:1], ps_s[b:b + 32, 0:256],
                                                                     axis=AX.X, negate=True))(hb),
                         waits=[s_done, C("dv")], inc="dv")
                sm2 = op("scalar", (lambda b: lambda E: E.activation(sexp[0:32, :], ps_s[b:b + 32, 0:256],
                                                                     AF.Exp, bias=smax[0:32, 0:1]))(hb),
                         waits=[sm1], inc="ac")
                bguard["s"] = sm2
                sm3 = op("vector", lambda E: E.reduce_sum(smax[0:32, 1:2], sexp[0:32, :], axis=AX.X),
                         waits=[sm2], inc="dv")
                sm4 = op("vector", lambda E: E.reciprocal(smax[0:32, 2:3], smax[0:32, 1:2]),
                         waits=[sm3], inc="dv")
                sm5 = op("vector", lambda E: E.scalar_tensor_tensor(abf[0:32, :], sexp[0:32, :],
                                                                    smax[0:32, 2:3], sexp[0:32, :],
                                                                    ALU.mult, ALU.bypass),
                         waits=[sm4], inc="dv")
                at_done = None
                for xt in range(2):
                    def mkat(x):
                        return lambda E: E.transpose(ps_t[0:128, x * 32:(x + 1) * 32],
                                                     abf[0:32, x * 128:(x + 1) * 128],
                                                     ident[0:32, 0:32])
                    m = op("tensor", mkat(xt), waits=[sm5, BG("t")], inc="mm")
                    def mkac(x):
                        return lambda E: E.tensor_copy(atr[:, x, :], ps_t[0:128, x * 32:(x + 1) * 32])
                    at_done = op("vector", mkac(xt), waits=[m], inc="dv")
                    bguard["t"] = at_done
                oev = []
                for c in range(4):
                    mi = None
                    for xt in range(2):
                        def mko(x, cc, hh_):
                            return lambda E: E.matmul(ps_y[0:32, cc % 2, cc // 2, :],
                                                      atr[:, x, :],
                                                      vgrp[:, x, cc * 8:(cc + 1) * 8,
                                                           hh_ * HD:(hh_ + 1) * HD],
                                                      start=(x == 0), stop=(x == 1))
                        mi = op("tensor", mko(xt, c, hh),
                                waits=[at_done, vld, BG("y%d%d" % (c % 2, c // 2)), osb_guard[0]]
                                if xt == 0 else (),
                                inc="mm" if xt == 1 else None)
                    vg_guard[0] = mi
                    def mkoe(cc):
                        return lambda E: E.activation(osb[0:32, cc * 512:(cc + 1) * 512],
                                                      ps_y[0:32, cc % 2, cc // 2, :], AF.Copy)
                    ev_o = op("scalar", mkoe(c), waits=[mi], inc="ac")
                    bguard["y%d%d" % (c % 2, c // 2)] = ev_o
                    oev.append(ev_o)
                osp = dma("scalar", "sot",
                          s_otok[:, head * HD:(head + 1) * HD].rearrange("(w r) h -> w r h", r=RG),
                          osb[0:32, :].rearrange("w (r h) -> w r h", r=RG),
                          waits=oev)
                osb_guard[0] = osp
                if tb == 0 and g == 0 and hh == 0:
                    if "q0" in dbg_out:
                        dma("gpsimd", "sdbg", dbg_out["q0"], qpair[:], waits=[q_ready])
                    if "wq0" in dbg_out:
                        dma("gpsimd", "sdbg", dbg_out["wq0"], wqTq[:, 0, :, :], waits=[q_ready])
                    if "xtb0" in dbg_out:
                        dma("gpsimd", "sdbg", dbg_out["xtb0"], xn_tb[:], waits=[q_ready])
                    if "a0" in dbg_out:
                        dma("gpsimd", "sdbg", dbg_out["a0"], abf[0:64, :], waits=[sm5])
                    if "sexp0" in dbg_out:
                        dma("gpsimd", "sdbg", dbg_out["sexp0"], sexp[0:64, :], waits=[sm5])
                    if "at0" in dbg_out:
                        dma("gpsimd", "sdbg", dbg_out["at0"], atr[:], waits=[at_done])
                    if "o0" in dbg_out:
                        dma("gpsimd", "sdbg", dbg_out["o0"], osb[0:32, :], waits=oev)
        heads_done = C("sot")

        ofm_done = None
        ofm_k_done = []
        sot_base = cnt["sot"] - 16 * 16
        for k in range(NKT):
            ofm_done = dmaT("sync", "sof", ofm[:, k, :], s_otok[:, k * 128:(k + 1) * 128],
                            waits=[("sot", sot_base + 16 * (2 * k + 2)), ofm_guard[0], ofm_done,
                                   last_t1[0]])
            last_t1[0] = ofm_done
            ofm_k_done.append(ofm_done)

        for half in range(2):
            lw = dma("scalar", "swp", wpT[:],
                     s_wpT[:, half * 512:(half + 1) * 512].rearrange("(k p) d -> p k d", p=128),
                     waits=wpT_done + [wpT_guard[0]])
            for tt in range(NT):
                xld = dma("scalar", "sxt%d" % (tt % 2), xtile[:, tt % 2, :],
                          xb[t0 + tt * 128: t0 + (tt + 1) * 128, half * 512:(half + 1) * 512],
                          waits=[xt_guard[tt % 2]])
                def mkpb(t, hf):
                    return lambda E: E.matmul(ps_y[:, t % 2, hf, :],
                                              wT[0:NE, t * 128:(t + 1) * 128],
                                              bdmat[0:NE, hf * 512:(hf + 1) * 512],
                                              start=True, stop=False)
                op("tensor", mkpb(tt, half), waits=[wT_done, bdm_done, lw,
                                                    BG("y%d%d" % (tt % 2, half))])
                mi = None
                for k in range(NKT):
                    def mkp(t, kk, hf):
                        return lambda E: E.matmul(ps_y[:, t % 2, hf, :],
                                                  ofm[:, kk, t * 128:(t + 1) * 128],
                                                  wpT[:, kk, :], start=False, stop=(kk == NKT - 1))
                    mi = op("tensor", mkp(tt, k, half), waits=[ofm_k_done[k]],
                            inc="mm" if k == NKT - 1 else None)
                ofm_guard[0] = mi
                wpT_guard[0] = mi
                def mkcv(t, hf):
                    return lambda E: E.tensor_tensor(combined[:, t, hf * 512:(hf + 1) * 512],
                                                     ps_y[:, t % 2, hf, :],
                                                     xtile[:, t % 2, :], ALU.add)
                cv = op("vector", mkcv(tt, half), waits=[mi, xld, comb_store[0]], inc="dv")
                bguard["y%d%d" % (tt % 2, half)] = cv
                xt_guard[tt % 2] = cv
        comb_init = C("dv")
        if "combe" in dbg_out and tb == 0:
            dma("sync", "sq", dbg_out["combe"][0], combined[:], waits=[comb_init])
        if "comb1" in dbg_out:
            dma("sync", "sq", dbg_out["comb1"][tb], combined[:], waits=[comb_init])
        if "otok" in dbg_out:
            for i in range(TQ // 128):
                ldb = dma("sync", "sq", vstage[:, 0, :], s_otok[i * 128:(i + 1) * 128, :],
                          waits=[ofm_done, C("dv"), C("ac")])
                cv2 = op("vector", lambda E: E.tensor_copy(xsq, vstage[:, 0, :]), waits=[ldb], inc="dv")
                dma("sync", "sq", dbg_out["otok"][tb][i * 128:(i + 1) * 128, :], xsq, waits=[cv2])

        for e in range(NE):
            if tb == 0:
                pump_prep(1)
            epar = e % 2
            bul = bu_done
            for grp in range(NGRP):
                gpar = (e * NGRP + grp) % 2
                wul = dma("scalar", "swu%d" % gpar, wug[:, gpar, :],
                          s_wuT[e][:, grp * G_F * NKT * 128:(grp + 1) * G_F * NKT * 128],
                          waits=wu_ready[e] + [wug_guard[gpar]])
                wdl = dma("scalar", "swd%d" % gpar, wdg[:, gpar, :, :],
                          s_wdT[e][grp * G_F * 128:(grp + 1) * G_F * 128, :]
                          .rearrange("(j p) d -> p j d", p=128),
                          waits=wd_ready[e] + [wdg_guard[gpar]])
                for tc in range(2):
                    up_done = []
                    for j in range(G_F):
                        jpar = j % 2
                        mi = None
                        for k in range(NKT):
                            def mku(jp, jj, kk, gp, cc):
                                return lambda E: E.matmul(
                                    ps_a[:, jp, :],
                                    wug[:, gp, (jj * NKT + kk) * 128:(jj * NKT + kk + 1) * 128],
                                    xn_tb[:, kk, cc * 512:(cc + 1) * 512],
                                    start=(kk == 0), stop=(kk == NKT - 1))
                            mi = op("tensor", mku(jpar, j, k, gpar, tc),
                                    waits=[wul, ld_xn, BG("a%d" % jpar), a1_guard[j]] if k == 0 else (),
                                    inc="mm" if k == NKT - 1 else None)
                        def mkgl(jp, jj, ee, gg):
                            return lambda E: E.activation(
                                a1g[:, jj, :], ps_a[:, jp, :], AF.Gelu,
                                bias=bug9[:, ee, gg * G_F + jj: gg * G_F + jj + 1])
                        gl = op("scalar", mkgl(jpar, j, e, grp), waits=[mi, bul], inc="ac")
                        bguard["a%d" % jpar] = gl
                        up_done.append(gl)
                    if tb == 0 and e == 0 and grp == 0 and tc == 0:
                        if "a10" in dbg_out:
                            dg = dma("gpsimd", "sdbg", dbg_out["a10"], a1g[:], waits=up_done)
                            a1_guard[0] = dg
                        if "wug0" in dbg_out:
                            dg2 = dma("gpsimd", "sdbg", dbg_out["wug0"], wug[:, gpar, :], waits=[wul] + up_done)
                            wug_guard[gpar] = dg2
                    for ts in range(4):
                        tile = tc * 4 + ts
                        for half in range(2):
                            mi = None
                            for j in range(G_F):
                                def mkd(jj, t_, hf, gp):
                                    return lambda E: E.matmul(
                                        ps_y[:, t_ % 2, hf, :],
                                        a1g[:, jj, t_ * 128:(t_ + 1) * 128],
                                        wdg[:, gp, jj, hf * 512:(hf + 1) * 512],
                                        start=(jj == 0), stop=(jj == G_F - 1))
                                mi = op("tensor", mkd(j, ts, half, gpar),
                                        waits=(up_done + [wdl, BG("y%d%d" % (ts % 2, half))])
                                        if j == 0 else (),
                                        inc="mm" if j == G_F - 1 else None)
                            def mkev(t_, hf, ee):
                                return lambda E: E.scalar_tensor_tensor(
                                    combined[:, t_, hf * 512:(hf + 1) * 512],
                                    ps_y[:, t_ % 2, hf, :],
                                    wgate[:, t_, ee:ee + 1],
                                    combined[:, t_, hf * 512:(hf + 1) * 512],
                                    ALU.mult, ALU.add)
                            bguard["y%d%d" % (ts % 2, half)] = op("vector", mkev(tile, half, e),
                               waits=[mi, comb_init, wgate_done], inc="dv")
                    for j in range(G_F):
                        a1_guard[j] = C("mm")
                wug_guard[gpar] = C("mm")
                wdg_guard[gpar] = C("mm")
            if "combe" in dbg_out and tb == 0:
                dma("sync", "sq", dbg_out["combe"][e + 1], combined[:], waits=[C("dv")])
        pb_done = C("dv")

        st = dma("sync", "syo", yout[t0:t0 + TQ, :].rearrange("(t p) d -> p t d", p=128),
                 combined[:], waits=[pb_done])
        comb_store[0] = st
        xn_tb_guard[0] = C("mm")

    if "wut0" in dbg_out:
        stg = xnout[:, 0, :]
        d2 = None
        for ch in range(FF * NKT // D):
            ldb = dma("sync", "sq", stg, s_wuT[3][:, ch * D:(ch + 1) * D],
                      waits=wu_ready[3] + [d2, C("dv")] if wu_ready[0] else [d2, C("dv")])
            cv = op("vector", lambda E: E.tensor_copy(xcent, stg), waits=[ldb], inc="dv")
            d2 = dma("sync", "sq", dbg_out["wut0"][:, ch * D:(ch + 1) * D], xcent, waits=[cv])
    if "wdt0" in dbg_out:
        d2 = None
        for ch in range(FF // 128):
            ldb = dma("sync", "sq", xnout[:, 0, :], s_wdT[3][ch * 128:(ch + 1) * 128, :],
                      waits=wd_ready[3] + [d2, C("dv")] if wd_ready[0] else [d2, C("dv")])
            cv = op("vector", lambda E: E.tensor_copy(xcent, xnout[:, 0, :]), waits=[ldb], inc="dv")
            d2 = dma("sync", "sq", dbg_out["wdt0"][ch * 128:(ch + 1) * 128, :], xcent, waits=[cv])

    op("sync", lambda E: E.nop(), waits=[C(s) for s in SEMNAMES])

    nc.all_engine_barrier()
    semnums = sorted(sems[s].num for s in SEMNAMES)
    assert semnums == list(range(semnums[0], semnums[0] + len(semnums))), semnums
    clr_range = range(semnums[0], semnums[-1] + 1)
    nc.gpsimd.dma_reset(clr_range)
    nc.gpsimd.sem_clear(clr_range)
    nc.all_engine_barrier()
    with nc.Block() as block:
        def mkrunner(eng_name):
            def runner(E):
                for waits, fn, inc, k in SCHED[eng_name]:
                    for sname, val in waits:
                        E.wait_ge(sems[sname], val)
                    ins = fn(E)
                    if inc is not None:
                        ins.then_inc(sems[inc], k)
            return runner
        block.tensor(mkrunner("tensor"))
        block.vector(mkrunner("vector"))
        block.scalar(mkrunner("scalar"))
        block.gpsimd(mkrunner("gpsimd"))
        block.sync(mkrunner("sync"))
    es.close()
    return nc, dbg_out


import numpy as np
from concourse.bass_utils import run_bass_kernel_spmd

_NC_CACHE = {}

def _get_nc():
    if "nc" not in _NC_CACHE:
        _NC_CACHE["nc"] = build(dbg=())[0]
    return _NC_CACHE["nc"]


def kernel(L0_attn_x, L0_attn_Wqkv, L0_attn_Wproj, L0_ffn_gate_w, L0_ffn_gate_bias,
           L0_ffn_Wu, L0_ffn_bu, L0_ffn_Wd, L0_ffn_bd, L0_ffn_sWu, L0_ffn_sbu,
           L0_ffn_sWd, L0_ffn_sbd):
    x = np.ascontiguousarray(np.asarray(L0_attn_x, dtype=np.float32))
    com = dict(
        wqkv=np.ascontiguousarray(np.asarray(L0_attn_Wqkv, np.float32)),
        wproj=np.ascontiguousarray(np.asarray(L0_attn_Wproj, np.float32)),
        gw=np.ascontiguousarray(np.asarray(L0_ffn_gate_w, np.float32)),
        gb=np.ascontiguousarray(np.asarray(L0_ffn_gate_bias, np.float32).reshape(1, 8)),
        wu=np.ascontiguousarray(np.asarray(L0_ffn_Wu, np.float32)),
        bu=np.ascontiguousarray(np.asarray(L0_ffn_bu, np.float32)),
        wd=np.ascontiguousarray(np.asarray(L0_ffn_Wd, np.float32)),
        bd=np.ascontiguousarray(np.asarray(L0_ffn_bd, np.float32)),
        swu=np.ascontiguousarray(np.asarray(L0_ffn_sWu, np.float32)),
        sbu=np.ascontiguousarray(np.asarray(L0_ffn_sbu, np.float32).reshape(1, FF)),
        swd=np.ascontiguousarray(np.asarray(L0_ffn_sWd, np.float32)),
        sbd=np.ascontiguousarray(np.asarray(L0_ffn_sbd, np.float32).reshape(1, D)),
    )
    in_maps = []
    for c in range(8):
        b, half = c // 2, c % 2
        xb_ = x[b]
        if half == 1:
            xb_ = np.concatenate([xb_[TLOC:], xb_[:TLOC]], axis=0)
        m = dict(com)
        m["xb"] = np.ascontiguousarray(xb_)
        in_maps.append(m)
    nc = _get_nc()
    res = run_bass_kernel_spmd(nc, in_maps, core_ids=list(range(8)))
    out = np.empty((B, NSEQ, D), dtype=np.float32)
    for c in range(8):
        b, half = c // 2, c % 2
        out[b, half * TLOC:(half + 1) * TLOC, :] = res.results[c]["yout"]
    return out


# version 09 (final)
# speedup vs baseline: 1.2115x; 1.1495x over previous
import contextlib
import concourse.bass as bass
import concourse.mybir as mybir

F32 = mybir.dt.float32
BF16 = mybir.dt.bfloat16
AF = mybir.ActivationFunctionType
ALU = mybir.AluOpType
AX = mybir.AxisListType

B, NSEQ, D = 4, 8192, 1024
H, HD = 16, 64
RG, XG = 32, 256
FF = 4096
NE = 9
NTB, TQ = 4, 1024
TLOC = NTB * TQ
NKT = D // 128
NT = TQ // 128
G_F = 4
NGRP = FF // 128 // G_F
NHP = H // 2

ENGS = ("tensor", "vector", "scalar", "gpsimd", "sync")


def build(dbg=()):
    nc = bass.Bass("TRN2", target_bir_lowering=False, debug=False)

    xb = nc.dram_tensor("xb", [NSEQ, D], F32, kind="ExternalInput").ap()
    wqkv = nc.dram_tensor("wqkv", [3 * D, D], F32, kind="ExternalInput").ap()
    wproj = nc.dram_tensor("wproj", [D, D], F32, kind="ExternalInput").ap()
    gw = nc.dram_tensor("gw", [8, D], F32, kind="ExternalInput").ap()
    gb = nc.dram_tensor("gb", [1, 8], F32, kind="ExternalInput").ap()
    wu_in = nc.dram_tensor("wu", [8, FF, D], F32, kind="ExternalInput").ap()
    bu_in = nc.dram_tensor("bu", [8, FF], F32, kind="ExternalInput").ap()
    wd_in = nc.dram_tensor("wd", [8, D, FF], F32, kind="ExternalInput").ap()
    bd_in = nc.dram_tensor("bd", [8, D], F32, kind="ExternalInput").ap()
    swu_in = nc.dram_tensor("swu", [FF, D], F32, kind="ExternalInput").ap()
    sbu_in = nc.dram_tensor("sbu", [1, FF], F32, kind="ExternalInput").ap()
    swd_in = nc.dram_tensor("swd", [D, FF], F32, kind="ExternalInput").ap()
    sbd_in = nc.dram_tensor("sbd", [1, D], F32, kind="ExternalInput").ap()

    yout = nc.dram_tensor("yout", [TLOC, D], F32, kind="ExternalOutput").ap()

    s_wq_bf = nc.dram_tensor("s_wq_bf", [3 * D, D], BF16, kind="Internal").ap()
    s_wp_bf = nc.dram_tensor("s_wp_bf", [D, D], BF16, kind="Internal").ap()
    s_wu_bf = nc.dram_tensor("s_wu_bf", [FF, D], BF16, kind="Internal").ap()
    s_wd_bf = nc.dram_tensor("s_wd_bf", [D, FF], BF16, kind="Internal").ap()
    s_wu_bf2 = nc.dram_tensor("s_wu_bf2", [2, NKT, FF, 128], BF16, kind="Internal").ap()
    s_wd_bf2 = nc.dram_tensor("s_wd_bf2", [2, FF // 128, D, 128], BF16, kind="Internal").ap()
    s_wqT = nc.dram_tensor("s_wqT", [D, 3 * D], BF16, kind="Internal").ap()
    s_wpT = nc.dram_tensor("s_wpT", [D, D], BF16, kind="Internal").ap()
    s_wuT = nc.dram_tensor("s_wuT", [NE, 128, FF * NKT], BF16, kind="Internal").ap()
    s_wdT = nc.dram_tensor("s_wdT", [NE, FF, D], BF16, kind="Internal").ap()
    s_xn_tok = nc.dram_tensor("s_xn_tok", [NSEQ, D], BF16, kind="Internal").ap()
    s_xn_fm = nc.dram_tensor("s_xn_fm", [D, NSEQ], BF16, kind="Internal").ap()
    s_kfm = nc.dram_tensor("s_kfm", [NHP, 128, NSEQ], BF16, kind="Internal").ap()
    s_vtok = nc.dram_tensor("s_vtok", [NSEQ, D], BF16, kind="Internal").ap()
    s_otok = nc.dram_tensor("s_otok", [TQ, D], BF16, kind="Internal").ap()

    dbg_out = {}
    def _dbg(name, shape):
        if name in dbg:
            dbg_out[name] = nc.dram_tensor("d_" + name, shape, F32, kind="ExternalOutput").ap()
    _dbg("xn", [NSEQ, D])
    _dbg("xnfm", [D, NSEQ])
    _dbg("wqt", [D, 3 * D])
    _dbg("kfm", [NHP, 128, NSEQ])
    _dbg("vtok", [NSEQ, D])
    _dbg("otok", [NTB, TQ, D])
    _dbg("w", [NTB, 128, NT, NE])
    _dbg("q0", [128, TQ])
    _dbg("wut0", [128, FF * NKT])
    _dbg("combe", [NE + 1, 128, NT, D])
    _dbg("a10", [128, G_F, 512])
    _dbg("wug0", [128, G_F * NKT * 128])
    _dbg("wdt0", [FF, D])
    _dbg("wq0", [128, NKT, 128])
    _dbg("xtb0", [128, NKT, TQ])
    _dbg("a0", [64, XG])
    _dbg("at0", [128, 2, RG])
    _dbg("o0", [32, RG * HD])
    _dbg("sexp0", [64, XG])
    _dbg("comb1", [NTB, 128, NT, D])

    es = contextlib.ExitStack()
    def sbt(name, shape, dt):
        return es.enter_context(nc.sbuf_tensor(name, shape, dt))
    def pst(name, shape, dt):
        return es.enter_context(nc.psum_tensor(name, shape, dt))

    combined = sbt("combined", [128, NT, D], F32)
    xn_tb = sbt("xn_tb", [128, NKT, TQ], BF16)
    tstage = sbt("tstage", [128, 2, 2048], BF16)
    kpair = sbt("kpair", [128, 2, NSEQ], BF16)
    qpair = sbt("qpair", [128, TQ], BF16)
    vgrp = sbt("vgrp", [128, 2, RG, 128], BF16)
    sexp = sbt("sexp", [64, XG], F32)
    smax = sbt("smax", [64, 4], F32)
    abf = sbt("abf", [64, XG], BF16)
    atr = sbt("atr", [128, 2, RG], BF16)
    osb = sbt("osb", [32, RG * HD], BF16)
    ofm = sbt("ofm", [128, NKT, TQ], BF16)
    wpT = sbt("wpT", [128, NKT, 512], BF16)
    xnstr = sbt("xnstr", [128, 2, NKT, 256], BF16)
    kvst = sbt("kvst", [128, 2, 512], BF16)
    vstage = sbt("vstage", [128, 2, D], BF16)
    wqTq = sbt("wqTq", [128, 2, NKT, 128], BF16)
    scst = sbt("scst", [128, NT, 8], F32)
    gtmp = sbt("gtmp", [128, NT, 8], F32)
    gtmp2 = sbt("gtmp2", [128, NT, 8], F32)
    gred = sbt("gred", [128, NT, 4], F32)
    wgate = sbt("wgate", [128, NT, NE], F32)
    wgate_bf = sbt("wgate_bf", [128, NT, NE], BF16)
    wT = sbt("wT", [NE, TQ], BF16)
    bdmat = sbt("bdmat", [NE, D], BF16)
    gbias = sbt("gbias", [1, 8], BF16)
    gw_bf = sbt("gw_bf", [8, D], BF16)
    gwT = sbt("gwT", [128, NKT, 8], BF16)
    ident = sbt("ident", [128, 128], BF16)
    ones_bf = sbt("ones_bf", [1, 128], BF16)
    stat = sbt("stat", [128, 8], F32)
    wug = sbt("wug", [128, 2, G_F * NKT * 128], BF16)
    wdg = sbt("wdg", [128, 2, G_F, D], BF16)
    xch = combined[:, 0:2, :]
    xcent = combined[:, 2, :]
    xsq = combined[:, 3, :]
    xnout = combined[:, 4:5, :].bitcast(BF16).rearrange("p o (a b) -> p (o a) b", a=2)
    a1g = sbt("a1g", [128, G_F, 512], BF16)
    bug9 = sbt("bug9", [128, NE, FF // 128], F32)
    bustage = sbt("bustage", [32, 128], F32)
    identf = sbt("identf", [128, 128], F32)
    xtile = sbt("xtile", [128, 2, 512], F32)

    wqTk_v = wug[:].rearrange("p b (j k c) -> p (b j) k c", k=NKT, c=128)
    wqTv_v = wdg[:].rearrange("p b j d -> p (b j) d")

    ps_a = pst("ps_a", [128, 2, 512], F32)
    ps_y = pst("ps_y", [128, 2, 2, 512], F32)
    ps_s = pst("ps_s", [128, 512], F32)
    ps_t = pst("ps_t", [128, 1024], BF16)

    SEMNAMES = ("mm", "dv", "ac", "gc", "sq", "aq", "gq",
                "tsti0", "tsti1", "tsto0", "tsto1", "sxp", "scu", "scd", "skv",
                "sot", "sof", "sxn", "swq0", "swq1", "skp", "svg0", "svg1",
                "sxs0", "sxs1", "swu0", "swu1", "swd0", "swd1", "swp",
                "sxt0", "sxt1", "sxc0", "sxc1", "sbu", "smisc", "syo", "sdbg")
    sems = {s: es.enter_context(nc.semaphore(s)) for s in SEMNAMES}
    cnt = {s: 0 for s in sems}
    SCHED = {e: [] for e in ENGS}
    last_wait = {e: {} for e in ENGS}

    def op(eng, fn, waits=(), inc=None, k=1):
        wl = []
        for wv in waits:
            if wv is None:
                continue
            sname, val = wv
            if val <= 0:
                continue
            if val > last_wait[eng].get(sname, -1):
                wl.append((sname, val))
                last_wait[eng][sname] = val
        SCHED[eng].append((tuple(wl), fn, inc, k))
        if inc is not None:
            cnt[inc] += k
            return (inc, cnt[inc])
        return None

    def dma(eng, sem, out_ap, in_ap, waits=(), **kw):
        def mk(o, i, kw2):
            return lambda E: E.dma_start(out=o, in_=i, **kw2)
        return op(eng, mk(out_ap, in_ap, kw), waits=waits, inc=sem, k=16)

    def dmaT(eng, sem, out_ap, in_ap, waits=()):
        def mk(o, i):
            return lambda E: E.dma_start_transpose(o, i)
        return op(eng, mk(out_ap, in_ap), waits=waits, inc=sem, k=16)

    def C(s):
        return (s, cnt[s])

    bguard = {}
    def BG(key):
        return bguard.get(key)

    def CT():
        return [("tsto0", cnt["tsto0"]), ("tsto1", cnt["tsto1"])]

    rr = [0]
    tst_guard = [None, None]
    last_t1 = [None]

    def transpose_stage(src_ap, dst_ap, ncols, waits):
        par = rr[0] % 2
        rr[0] += 1
        w = list(waits) + [tst_guard[par], last_t1[0]]
        t1 = dmaT("sync", "tsti%d" % par, tstage[:, par, 0:ncols], src_ap, waits=w)
        last_t1[0] = t1
        t2 = dma("gpsimd", "tsto%d" % par, dst_ap, tstage[:, par, 0:ncols], waits=[t1])
        tst_guard[par] = t2
        return t2

    i0 = op("gpsimd", lambda E: E.memset(ident[:], 1.0), inc="gc")
    ident_done = op("gpsimd", lambda E: E.affine_select(ident[:], ident[:], [[1, 128]],
                                                        ALU.is_equal, 0.0, base=0,
                                                        channel_multiplier=-1),
                    waits=[i0], inc="gc")
    op("gpsimd", lambda E: E.memset(ones_bf[:], 1.0), inc="gc")
    ones_done = C("gc")
    dma("gpsimd", "smisc", gbias[:], gb)
    dma("gpsimd", "smisc", bdmat[0:8, :], bd_in)
    dma("gpsimd", "smisc", bdmat[8:9, :], sbd_in)
    dma("gpsimd", "smisc", gw_bf[:], gw)
    misc_done = C("smisc")
    bdm_done = misc_done
    cgw = misc_done

    idf = op("vector", lambda E: E.tensor_copy(identf[:], ident[:]), waits=[ident_done], inc="dv")
    bu_done = None
    for e in range(NE):
        _src = (bu_in[e:e + 1, :] if e < 8 else sbu_in).rearrange("o (j p) -> (o j) p", p=128)
        bl = dma("gpsimd", "sbu", bustage[:], _src, waits=[bu_done])
        def mkbt(ee):
            return lambda E: E.transpose(ps_s[0:128, 256:288].bitcast(F32) if False else ps_s[0:128, 256:288],
                                         bustage[:], identf[0:32, 0:32])
        mt = op("tensor", mkbt(e), waits=[bl, idf, BG("s")], inc="mm")
        def mkbc(ee):
            return lambda E: E.tensor_copy(bug9[:, ee, :], ps_s[0:128, 256:288])
        bu_done = op("vector", mkbc(e), waits=[mt], inc="dv")
        bguard["s"] = bu_done
    cw = dma("gpsimd", "scu", s_wq_bf, wqkv)
    for k in range(NKT):
        for hh in range(2):
            transpose_stage(s_wq_bf[hh * 1536:(hh + 1) * 1536, k * 128:(k + 1) * 128],
                            s_wqT[k * 128:(k + 1) * 128, hh * 1536:(hh + 1) * 1536],
                            1536, [cw])
    wqT_done = CT()
    cw = dma("gpsimd", "scu", s_wp_bf, wproj, waits=wqT_done)
    for k in range(NKT):
        for hh in range(2):
            transpose_stage(s_wp_bf[hh * 512:(hh + 1) * 512, k * 128:(k + 1) * 128],
                            s_wpT[k * 128:(k + 1) * 128, hh * 512:(hh + 1) * 512],
                            512, [cw])
    wpT_done = CT()
    for k in range(NKT):
        def mk_gwt(kk):
            return lambda E: E.transpose(ps_t[0:128, 0:8], gw_bf[0:8, kk * 128:(kk + 1) * 128],
                                         ident[0:8, 0:8])
        m = op("tensor", mk_gwt(k), waits=[cgw, ident_done, BG("t")], inc="mm")
        def mk_gwc(kk):
            return lambda E: E.tensor_copy(gwT[:, kk, :], ps_t[0:128, 0:8])
        bguard["t"] = op("vector", mk_gwc(k), waits=[m], inc="dv")
    gwT_done = C("dv")

    wu_ready = [None] * NE
    wd_ready = [None] * NE
    wuT_view = s_wuT.rearrange("e p (t k c) -> e p t k c", k=NKT, c=128)

    def prep_expert(e):
        cu = dma("gpsimd", "scu", s_wu_bf, (wu_in[e] if e < 8 else swu_in),
                 waits=(wu_ready[e - 1] if e > 0 else wpT_done))
        cd = dma("gpsimd", "scd", s_wd_bf, (wd_in[e] if e < 8 else swd_in),
                 waits=(wd_ready[e - 1] if e > 0 else []))
        for k in range(NKT):
            for hh in range(2):
                dst = wuT_view[e][:, hh * 16:(hh + 1) * 16, k, :]
                srca = s_wu_bf[hh * 2048:(hh + 1) * 2048, k * 128:(k + 1) * 128]
                transpose_stage(srca, dst, 2048, [cu])
        wu_ready[e] = CT()
        for f in range(FF // 128):
            transpose_stage(s_wd_bf[:, f * 128:(f + 1) * 128],
                            s_wdT[e][f * 128:(f + 1) * 128, :], D, [cd])
        wd_ready[e] = CT()
    prep_queue = list(range(NE))

    def pump_prep(n=1):
        for _ in range(n):
            if prep_queue:
                prep_expert(prep_queue.pop(0))

    xch_guard = [None, None]
    xn_last = None
    for i in range(NSEQ // 128):
        par = i % 2
        ld = dma("sync", "sxc%d" % par, xch[:, par, :], xb[i * 128:(i + 1) * 128, :],
                 waits=[xch_guard[par]])
        def _r1(p):
            return lambda E: E.reduce_sum(stat[:, 0:1], xch[:, p, :], axis=AX.X)
        r1 = op("vector", _r1(par), waits=[ld, C("dv")], inc="dv")
        r2 = op("vector", lambda E: E.tensor_scalar(stat[:, 1:2], stat[:, 0:1], 1.0 / D, None, ALU.mult),
                waits=[r1], inc="dv")
        def _r3(p):
            return lambda E: E.scalar_tensor_tensor(xcent, xch[:, p, :], stat[:, 1:2],
                                                    xch[:, p, :], ALU.subtract, ALU.bypass)
        r3 = op("vector", _r3(par), waits=[r2], inc="dv")
        xch_guard[par] = r3
        r4 = op("scalar", lambda E: E.activation(xsq, xcent, AF.Square), waits=[r3, C("ac")], inc="ac")
        r5 = op("vector", lambda E: E.reduce_sum(stat[:, 2:3], xsq, axis=AX.X), waits=[r4], inc="dv")
        r6 = op("vector", lambda E: E.tensor_scalar(stat[:, 3:4], stat[:, 2:3], 1.0 / D, 1.0e-5,
                                                    ALU.mult, ALU.add), waits=[r5], inc="dv")
        r7 = op("scalar", lambda E: E.activation(stat[:, 4:5], stat[:, 3:4], AF.Sqrt), waits=[r6], inc="ac")
        r8 = op("vector", lambda E: E.reciprocal(stat[:, 5:6], stat[:, 4:5]), waits=[r7], inc="dv")
        def _r9(p):
            return lambda E: E.scalar_tensor_tensor(xnout[:, p, :], xcent, stat[:, 5:6],
                                                    xcent, ALU.mult, ALU.bypass)
        r9 = op("vector", _r9(par), waits=[r8, ("sxp", 16 * (i - 1))], inc="dv")
        xn_last = dma("gpsimd", "sxp", s_xn_tok[i * 128:(i + 1) * 128, :], xnout[:, par, :], waits=[r9])
    xn_tok_done = xn_last
    ln_done = [("sxp", 16 * 16 * (ch + 1)) for ch in range(4)]

    if "xn" in dbg_out:
        for i in range(NSEQ // 128):
            ldb = dma("sync", "sq", xnout[:, 0, :], s_xn_tok[i * 128:(i + 1) * 128, :],
                      waits=[xn_tok_done, C("dv")])
            cv = op("vector", lambda E: E.tensor_copy(xcent, xnout[:, 0, :]), waits=[ldb], inc="dv")
            dma("sync", "sq", dbg_out["xn"][i * 128:(i + 1) * 128, :], xcent, waits=[cv])

    xnq_done = []
    for ch in range(4):
        for k in range(NKT):
            transpose_stage(
                s_xn_tok[ch * 2048:(ch + 1) * 2048, k * 128:(k + 1) * 128],
                s_xn_fm[k * 128:(k + 1) * 128, ch * 2048:(ch + 1) * 2048],
                2048, [ln_done[ch]])
        xnq_done.append(CT())
    xnfm_done = CT()

    pump_prep(2)
    if "xnfm" in dbg_out:
        for i in range(D // 128):
            ldb = dma("sync", "sq", xnout[:, 0, :].bitcast(BF16) if False else xnout[:, 0, :],
                      s_xn_fm[i * 128:(i + 1) * 128, 0:D], waits=xnfm_done + [C("dv")])
            cv = op("vector", lambda E: E.tensor_copy(xcent, xnout[:, 0, :]), waits=[ldb], inc="dv")
            d2 = dma("sync", "sq", dbg_out["xnfm"][i * 128:(i + 1) * 128, 0:D], xcent, waits=[cv])
            for ch in range(1, NSEQ // D):
                ldb = dma("sync", "sq", xnout[:, 0, :], s_xn_fm[i * 128:(i + 1) * 128, ch * D:(ch + 1) * D],
                          waits=[d2, C("dv")])
                cv = op("vector", lambda E: E.tensor_copy(xcent, xnout[:, 0, :]), waits=[ldb], inc="dv")
                d2 = dma("sync", "sq", dbg_out["xnfm"][i * 128:(i + 1) * 128, ch * D:(ch + 1) * D],
                         xcent, waits=[cv])
    if "wqt" in dbg_out:
        for i in range(D // 128):
            d2 = None
            for ch in range(3):
                ldb = dma("sync", "sq", xnout[:, 0, :], s_wqT[i * 128:(i + 1) * 128, ch * D:(ch + 1) * D],
                          waits=wqT_done + [d2, C("dv")])
                cv = op("vector", lambda E: E.tensor_copy(xcent, xnout[:, 0, :]), waits=[ldb], inc="dv")
                d2 = dma("sync", "sq", dbg_out["wqt"][i * 128:(i + 1) * 128, ch * D:(ch + 1) * D],
                         xcent, waits=[cv])
    wug_flat = wug[:].rearrange("p b f -> p (b f)")
    l_k = None
    for g_ in range(NHP):
        l_k = dma("scalar", "swu0",
                  wug_flat[:, g_ * (NKT * 128):(g_ + 1) * (NKT * 128)]
                  .rearrange("p (k c) -> p k c", c=128),
                  s_wqT[:, D + g_ * 128:D + (g_ + 1) * 128].rearrange("(k p) c -> p k c", p=128),
                  waits=wqT_done)
    l_v = dma("scalar", "swd0", wqTv_v,
              s_wqT[:, 2 * D:3 * D].rearrange("(k p) v -> p k v", p=128),
              waits=wqT_done)

    xnstr_guard = [None, None]
    kvev_guard = [None, None]
    vst_guard = [None, None]
    wug_guard = [None, None]
    wdg_guard = [None, None]
    for tcx in range(NSEQ // 256):
        par = tcx % 2
        ldx = dma("scalar", "sxs%d" % par, xnstr[:, par, :, :],
                  s_xn_fm[:, tcx * 256:(tcx + 1) * 256].rearrange("(k p) t -> p k t", p=128),
                  waits=xnq_done[tcx // 8] + [xnstr_guard[par]])
        last_consumer = None
        for g in range(NHP):
            mi = None
            for k in range(NKT):
                def mk(gg, kk, p):
                    return lambda E: E.matmul(ps_a[:, gg % 2, 0:256], wqTk_v[:, gg, kk, :],
                                              xnstr[:, p, kk, :],
                                              start=(kk == 0), stop=(kk == NKT - 1))
                mi = op("tensor", mk(g, k, par),
                        waits=[ldx, l_k, BG("a%d" % (g % 2))] if k == 0 else (),
                        inc="mm" if k == NKT - 1 else None)
            def mkev(gg):
                return lambda E: E.activation(kvst[:, gg % 2, 0:256], ps_a[:, gg % 2, 0:256], AF.Copy)
            ev = op("scalar", mkev(g), waits=[mi, kvev_guard[g % 2]], inc="ac")
            bguard["a%d" % (g % 2)] = ev
            sp = dma("scalar", "skv", s_kfm[g][:, tcx * 256:(tcx + 1) * 256], kvst[:, g % 2, 0:256],
                     waits=[ev])
            kvev_guard[g % 2] = sp
            last_consumer = mi
        for ts in range(2):
            evs = []
            for half in range(2):
                mi = None
                for k in range(NKT):
                    def mkv(hh, kk, p, t):
                        return lambda E: E.matmul(ps_y[:, t, hh, :],
                                                  xnstr[:, p, kk, t * 128:(t + 1) * 128],
                                                  wqTv_v[:, kk, hh * 512:(hh + 1) * 512],
                                                  start=(kk == 0), stop=(kk == NKT - 1))
                    mi = op("tensor", mkv(half, k, par, ts),
                            waits=[ldx, l_v, BG("y%d0" % ts), BG("y%d1" % ts)]
                            if (k == 0 and half == 0) else (),
                            inc="mm" if k == NKT - 1 else None)
                def mkve(hh, t):
                    return lambda E: E.activation(vstage[:, t, hh * 512:(hh + 1) * 512],
                                                  ps_y[:, t, hh, :], AF.Copy)
                ev_v = op("scalar", mkve(half, ts), waits=[mi, vst_guard[ts]], inc="ac")
                bguard["y%d%d" % (ts, half)] = ev_v
                evs.append(ev_v)
            sp = dma("scalar", "skv", s_vtok[tcx * 256 + ts * 128: tcx * 256 + (ts + 1) * 128, :],
                     vstage[:, ts, :], waits=evs)
            vst_guard[ts] = sp
            last_consumer = mi
        xnstr_guard[par] = last_consumer
    kv_done_all = [C("skv")]
    wug_guard[0] = wug_guard[1] = C("mm")
    wdg_guard[0] = wdg_guard[1] = C("mm")

    if "kfm" in dbg_out:
        for g in range(NHP):
            for ch in range(NSEQ // 512):
                ldb = dma("sync", "sq", kvst[:, 0, :], s_kfm[g][:, ch * 512:(ch + 1) * 512],
                          waits=kv_done_all + [C("dv")])
                cv = op("vector", lambda E: E.tensor_copy(xsq[:, 0:512], kvst[:, 0, :]),
                        waits=[ldb], inc="dv")
                dma("sync", "sq", dbg_out["kfm"][g][:, ch * 512:(ch + 1) * 512], xsq[:, 0:512],
                    waits=[cv])
    if "vtok" in dbg_out:
        for i in range(NSEQ // 128):
            ldb = dma("sync", "sq", vstage[:, 0, :], s_vtok[i * 128:(i + 1) * 128, :],
                      waits=kv_done_all + [C("dv"), C("ac")])
            cv = op("vector", lambda E: E.tensor_copy(xsq, vstage[:, 0, :]), waits=[ldb], inc="dv")
            dma("sync", "sq", dbg_out["vtok"][i * 128:(i + 1) * 128, :], xsq, waits=[cv])

    xn_tb_guard = [None]
    comb_store = [None]
    kp_guard = [None, None]
    q_guard = [None]
    vg_guard = [None, None]
    wq_guard = [None, None]
    osb_guard = [None]
    ofm_guard = [None]
    wpT_guard = [None]
    xt_guard = [None, None]
    a1_guard = [None] * G_F

    for tb in range(NTB):
        t0 = tb * TQ

        ld_xn = dma("scalar", "sxn", xn_tb[:],
                    s_xn_fm[:, t0:t0 + TQ].rearrange("(k p) t -> p k t", p=128),
                    waits=xnq_done[tb // 2] + [xn_tb_guard[0]])

        for tt in range(NT):
            def mkb(t):
                return lambda E: E.matmul(ps_a[:, t % 2, 0:8], ones_bf[0:1, 0:128], gbias[0:1, :],
                                          start=True, stop=False)
            op("tensor", mkb(tt), waits=[ld_xn, gwT_done, ones_done, BG("a%d" % (tt % 2))])
            mi = None
            for k in range(NKT):
                def mkg(t, kk):
                    return lambda E: E.matmul(ps_a[:, t % 2, 0:8], xn_tb[:, kk, t * 128:(t + 1) * 128],
                                              gwT[:, kk, :], start=False, stop=(kk == NKT - 1))
                mi = op("tensor", mkg(tt, k), inc="mm" if k == NKT - 1 else None)
            def mks(t):
                return lambda E: E.activation(scst[:, t, :], ps_a[:, t % 2, 0:8], AF.Sigmoid)
            bguard["a%d" % (tt % 2)] = op("scalar", mks(tt), waits=[mi], inc="ac")
        gate_sc = C("ac")

        def bget(apc):
            return apc.rearrange("p (g o) -> p g o", o=1).broadcast_to([128, NT, 8])
        g1 = op("vector", lambda E: E.reduce_max(gred[:, :, 0], scst[:], axis=AX.X),
                waits=[gate_sc, C("dv")], inc="dv")
        g2 = op("vector", lambda E: E.tensor_tensor(gtmp[:], scst[:], bget(gred[:, :, 0]), ALU.is_equal),
                waits=[g1], inc="dv")
        g3 = op("vector", lambda E: E.scalar_tensor_tensor(gtmp2[:], gtmp[:], -1.0e30, scst[:],
                                                           ALU.mult, ALU.add), waits=[g2], inc="dv")
        g4 = op("vector", lambda E: E.reduce_max(gred[:, :, 1], gtmp2[:], axis=AX.X), waits=[g3], inc="dv")
        g5 = op("vector", lambda E: E.tensor_tensor(gtmp[:], scst[:], bget(gred[:, :, 1]), ALU.is_ge),
                waits=[g4], inc="dv")
        g6 = op("vector", lambda E: E.tensor_tensor(gred[:, :, 2], gred[:, :, 0], gred[:, :, 1], ALU.add),
                waits=[g5], inc="dv")
        g7 = op("vector", lambda E: E.tensor_scalar(gred[:, :, 2], gred[:, :, 2], 1.0e-6, None, ALU.add),
                waits=[g6], inc="dv")
        g8 = op("vector", lambda E: E.reciprocal(gred[:, :, 3], gred[:, :, 2]), waits=[g7], inc="dv")
        g9 = op("vector", lambda E: E.tensor_tensor(gtmp2[:], scst[:], gtmp[:], ALU.mult), waits=[g8], inc="dv")
        g10 = op("vector", lambda E: E.tensor_tensor(wgate[:, :, 0:8], gtmp2[:], bget(gred[:, :, 3]), ALU.mult),
                 waits=[g9], inc="dv")
        g11 = op("vector", lambda E: E.memset(wgate[:, :, 8:9], 0.1), waits=[g10], inc="dv")
        g12 = op("vector", lambda E: E.tensor_copy(wgate_bf[:], wgate[:]), waits=[g11], inc="dv")
        wgate_done = g11
        if "w" in dbg_out:
            dma("sync", "sq", dbg_out["w"][tb], wgate[:], waits=[wgate_done])
        wT_last = None
        for tt in range(NT):
            def mkt(t):
                return lambda E: E.transpose(ps_t[0:NE, 128:256], wgate_bf[:, t, :], ident[:])
            m = op("tensor", mkt(tt), waits=[g12, BG("t"), ident_done], inc="mm")
            def mkc(t):
                return lambda E: E.tensor_copy(wT[0:NE, t * 128:(t + 1) * 128], ps_t[0:NE, 128:256])
            wT_last = op("vector", mkc(tt), waits=[m], inc="dv")
            bguard["t"] = wT_last
        wT_done = wT_last

        for g in range(NHP):
            kld = dma("scalar", "skp", kpair[:, g % 2, :], s_kfm[g],
                      waits=kv_done_all + [kp_guard[g % 2]])
            qw = dma("scalar", "swq%d" % (g % 2), wqTq[:, g % 2, :, :],
                     s_wqT[:, g * 128:(g + 1) * 128].rearrange("(k p) c -> p k c", p=128),
                     waits=wqT_done + [wq_guard[g % 2]])
            q_ready = None
            for ch in range(TQ // 512):
                mi = None
                for k in range(NKT):
                    def mkq(gg, kk, cc):
                        return lambda E: E.matmul(ps_a[:, cc % 2, :],
                                                  wqTq[:, gg % 2, kk, :],
                                                  xn_tb[:, kk, cc * 512:(cc + 1) * 512],
                                                  start=(kk == 0), stop=(kk == NKT - 1))
                    mi = op("tensor", mkq(g, k, ch),
                            waits=[qw, ld_xn, q_guard[0], BG("a%d" % (ch % 2))] if k == 0 else (),
                            inc="mm" if k == NKT - 1 else None)
                wq_guard[g % 2] = mi
                def mkqe(cc):
                    return lambda E: E.activation(qpair[:, cc * 512:(cc + 1) * 512], ps_a[:, cc % 2, :],
                                                  AF.Copy, scale=0.125)
                q_ready = op("scalar", mkqe(ch), waits=[mi], inc="ac")
                bguard["a%d" % (ch % 2)] = q_ready

            if tb == 0 and g in (2, 4, 6):
                pump_prep(1)
            vld_pair = None
            for xt in range(2):
                vld_pair = dma("scalar", "svg0",
                               vgrp[:, xt, :, :],
                               s_vtok[xt * 4096:(xt + 1) * 4096, g * 128:(g + 1) * 128]
                               .rearrange("(x r) c -> x r c", r=RG),
                               waits=(kv_done_all + [vg_guard[0]]) if xt == 0 else ())
            for hh in range(2):
                head = 2 * g + hh
                hb = hh * 64
                vld = vld_pair
                mi = None
                for r in range(RG):
                    def mks2(b, rr_, kb_unused):
                        return lambda E: E.matmul(ps_s[b:b + 32, 0:256],
                                                  qpair[b:b + 64, rr_::RG],
                                                  kpair[b:b + 64, kb_unused, rr_::RG],
                                                  start=(rr_ == 0), stop=(rr_ == RG - 1))
                    mi = op("tensor", mks2(hb, r, g % 2),
                            waits=[q_ready, kld, BG("s")] if r == 0 else (),
                            inc="mm" if r == RG - 1 else None)
                s_done = mi
                kp_guard[g % 2] = mi
                q_guard[0] = mi
                sm1 = op("vector", (lambda b: lambda E: E.reduce_max(smax[0:32, 0:1], ps_s[b:b + 32, 0:256],
                                                                     axis=AX.X, negate=True))(hb),
                         waits=[s_done, C("dv")], inc="dv")
                sm2 = op("scalar", (lambda b: lambda E: E.activation(sexp[0:32, :], ps_s[b:b + 32, 0:256],
                                                                     AF.Exp, bias=smax[0:32, 0:1]))(hb),
                         waits=[sm1], inc="ac")
                bguard["s"] = sm2
                sm3 = op("vector", lambda E: E.reduce_sum(smax[0:32, 1:2], sexp[0:32, :], axis=AX.X),
                         waits=[sm2], inc="dv")
                sm4 = op("vector", lambda E: E.reciprocal(smax[0:32, 2:3], smax[0:32, 1:2]),
                         waits=[sm3], inc="dv")
                sm5 = op("vector", lambda E: E.scalar_tensor_tensor(abf[0:32, :], sexp[0:32, :],
                                                                    smax[0:32, 2:3], sexp[0:32, :],
                                                                    ALU.mult, ALU.bypass),
                         waits=[sm4], inc="dv")
                at_done = None
                for xt in range(2):
                    def mkat(x):
                        return lambda E: E.transpose(ps_t[0:128, x * 32:(x + 1) * 32],
                                                     abf[0:32, x * 128:(x + 1) * 128],
                                                     ident[0:32, 0:32])
                    m = op("tensor", mkat(xt), waits=[sm5, BG("t")], inc="mm")
                    def mkac(x):
                        return lambda E: E.tensor_copy(atr[:, x, :], ps_t[0:128, x * 32:(x + 1) * 32])
                    at_done = op("vector", mkac(xt), waits=[m], inc="dv")
                    bguard["t"] = at_done
                oev = []
                for c in range(4):
                    mi = None
                    for xt in range(2):
                        def mko(x, cc, hh_):
                            return lambda E: E.matmul(ps_y[0:32, cc % 2, cc // 2, :],
                                                      atr[:, x, :],
                                                      vgrp[:, x, cc * 8:(cc + 1) * 8,
                                                           hh_ * HD:(hh_ + 1) * HD],
                                                      start=(x == 0), stop=(x == 1))
                        mi = op("tensor", mko(xt, c, hh),
                                waits=[at_done, vld, BG("y%d%d" % (c % 2, c // 2)), osb_guard[0]]
                                if xt == 0 else (),
                                inc="mm" if xt == 1 else None)
                    vg_guard[0] = mi
                    def mkoe(cc):
                        return lambda E: E.activation(osb[0:32, cc * 512:(cc + 1) * 512],
                                                      ps_y[0:32, cc % 2, cc // 2, :], AF.Copy)
                    ev_o = op("scalar", mkoe(c), waits=[mi], inc="ac")
                    bguard["y%d%d" % (c % 2, c // 2)] = ev_o
                    oev.append(ev_o)
                osp = dma("scalar", "sot",
                          s_otok[:, head * HD:(head + 1) * HD].rearrange("(w r) h -> w r h", r=RG),
                          osb[0:32, :].rearrange("w (r h) -> w r h", r=RG),
                          waits=oev)
                osb_guard[0] = osp
                if tb == 0 and g == 0 and hh == 0:
                    if "q0" in dbg_out:
                        dma("gpsimd", "sdbg", dbg_out["q0"], qpair[:], waits=[q_ready])
                    if "wq0" in dbg_out:
                        dma("gpsimd", "sdbg", dbg_out["wq0"], wqTq[:, 0, :, :], waits=[q_ready])
                    if "xtb0" in dbg_out:
                        dma("gpsimd", "sdbg", dbg_out["xtb0"], xn_tb[:], waits=[q_ready])
                    if "a0" in dbg_out:
                        dma("gpsimd", "sdbg", dbg_out["a0"], abf[0:64, :], waits=[sm5])
                    if "sexp0" in dbg_out:
                        dma("gpsimd", "sdbg", dbg_out["sexp0"], sexp[0:64, :], waits=[sm5])
                    if "at0" in dbg_out:
                        dma("gpsimd", "sdbg", dbg_out["at0"], atr[:], waits=[at_done])
                    if "o0" in dbg_out:
                        dma("gpsimd", "sdbg", dbg_out["o0"], osb[0:32, :], waits=oev)
        heads_done = C("sot")

        ofm_done = None
        ofm_k_done = []
        sot_base = cnt["sot"] - 16 * 16
        for k in range(NKT):
            ofm_done = dmaT("sync", "sof", ofm[:, k, :], s_otok[:, k * 128:(k + 1) * 128],
                            waits=[("sot", sot_base + 16 * (2 * k + 2)), ofm_guard[0], ofm_done,
                                   last_t1[0]])
            last_t1[0] = ofm_done
            ofm_k_done.append(ofm_done)

        for half in range(2):
            lw = dma("scalar", "swp", wpT[:],
                     s_wpT[:, half * 512:(half + 1) * 512].rearrange("(k p) d -> p k d", p=128),
                     waits=wpT_done + [wpT_guard[0]])
            for tt in range(NT):
                xld = dma("scalar", "sxt%d" % (tt % 2), xtile[:, tt % 2, :],
                          xb[t0 + tt * 128: t0 + (tt + 1) * 128, half * 512:(half + 1) * 512],
                          waits=[xt_guard[tt % 2]])
                def mkpb(t, hf):
                    return lambda E: E.matmul(ps_y[:, t % 2, hf, :],
                                              wT[0:NE, t * 128:(t + 1) * 128],
                                              bdmat[0:NE, hf * 512:(hf + 1) * 512],
                                              start=True, stop=False)
                op("tensor", mkpb(tt, half), waits=[wT_done, bdm_done, lw,
                                                    BG("y%d%d" % (tt % 2, half))])
                mi = None
                for k in range(NKT):
                    def mkp(t, kk, hf):
                        return lambda E: E.matmul(ps_y[:, t % 2, hf, :],
                                                  ofm[:, kk, t * 128:(t + 1) * 128],
                                                  wpT[:, kk, :], start=False, stop=(kk == NKT - 1))
                    mi = op("tensor", mkp(tt, k, half), waits=[ofm_k_done[k]],
                            inc="mm" if k == NKT - 1 else None)
                ofm_guard[0] = mi
                wpT_guard[0] = mi
                def mkcv(t, hf):
                    return lambda E: E.tensor_tensor(combined[:, t, hf * 512:(hf + 1) * 512],
                                                     ps_y[:, t % 2, hf, :],
                                                     xtile[:, t % 2, :], ALU.add)
                cv = op("vector", mkcv(tt, half), waits=[mi, xld, comb_store[0]], inc="dv")
                bguard["y%d%d" % (tt % 2, half)] = cv
                xt_guard[tt % 2] = cv
        comb_init = C("dv")
        if "combe" in dbg_out and tb == 0:
            dma("sync", "sq", dbg_out["combe"][0], combined[:], waits=[comb_init])
        if "comb1" in dbg_out:
            dma("sync", "sq", dbg_out["comb1"][tb], combined[:], waits=[comb_init])
        if "otok" in dbg_out:
            for i in range(TQ // 128):
                ldb = dma("sync", "sq", vstage[:, 0, :], s_otok[i * 128:(i + 1) * 128, :],
                          waits=[ofm_done, C("dv"), C("ac")])
                cv2 = op("vector", lambda E: E.tensor_copy(xsq, vstage[:, 0, :]), waits=[ldb], inc="dv")
                dma("sync", "sq", dbg_out["otok"][tb][i * 128:(i + 1) * 128, :], xsq, waits=[cv2])

        for e in range(NE):
            if tb == 0:
                pump_prep(1)
            epar = e % 2
            bul = bu_done
            for grp in range(NGRP):
                gpar = (e * NGRP + grp) % 2
                wul = dma("scalar", "swu%d" % gpar, wug[:, gpar, :],
                          s_wuT[e][:, grp * G_F * NKT * 128:(grp + 1) * G_F * NKT * 128],
                          waits=wu_ready[e] + [wug_guard[gpar]])
                wdl = dma("scalar", "swd%d" % gpar, wdg[:, gpar, :, :],
                          s_wdT[e][grp * G_F * 128:(grp + 1) * G_F * 128, :]
                          .rearrange("(j p) d -> p j d", p=128),
                          waits=wd_ready[e] + [wdg_guard[gpar]])
                for tc in range(2):
                    up_done = []
                    for j in range(G_F):
                        jpar = j % 2
                        mi = None
                        for k in range(NKT):
                            def mku(jp, jj, kk, gp, cc):
                                return lambda E: E.matmul(
                                    ps_a[:, jp, :],
                                    wug[:, gp, (jj * NKT + kk) * 128:(jj * NKT + kk + 1) * 128],
                                    xn_tb[:, kk, cc * 512:(cc + 1) * 512],
                                    start=(kk == 0), stop=(kk == NKT - 1))
                            mi = op("tensor", mku(jpar, j, k, gpar, tc),
                                    waits=[wul, ld_xn, BG("a%d" % jpar), a1_guard[j]] if k == 0 else (),
                                    inc="mm" if k == NKT - 1 else None)
                        def mkgl(jp, jj, ee, gg):
                            return lambda E: E.activation(
                                a1g[:, jj, :], ps_a[:, jp, :], AF.Gelu,
                                bias=bug9[:, ee, gg * G_F + jj: gg * G_F + jj + 1])
                        gl = op("scalar", mkgl(jpar, j, e, grp), waits=[mi, bul], inc="ac")
                        bguard["a%d" % jpar] = gl
                        up_done.append(gl)
                    if tb == 0 and e == 0 and grp == 0 and tc == 0:
                        if "a10" in dbg_out:
                            dg = dma("gpsimd", "sdbg", dbg_out["a10"], a1g[:], waits=up_done)
                            a1_guard[0] = dg
                        if "wug0" in dbg_out:
                            dg2 = dma("gpsimd", "sdbg", dbg_out["wug0"], wug[:, gpar, :], waits=[wul] + up_done)
                            wug_guard[gpar] = dg2
                    for ts in range(4):
                        tile = tc * 4 + ts
                        for half in range(2):
                            mi = None
                            for j in range(G_F):
                                def mkd(jj, t_, hf, gp):
                                    return lambda E: E.matmul(
                                        ps_y[:, t_ % 2, hf, :],
                                        a1g[:, jj, t_ * 128:(t_ + 1) * 128],
                                        wdg[:, gp, jj, hf * 512:(hf + 1) * 512],
                                        start=(jj == 0), stop=(jj == G_F - 1))
                                mi = op("tensor", mkd(j, ts, half, gpar),
                                        waits=(up_done + [wdl, BG("y%d%d" % (ts % 2, half))])
                                        if j == 0 else (),
                                        inc="mm" if j == G_F - 1 else None)
                            def mkev(t_, hf, ee):
                                return lambda E: E.scalar_tensor_tensor(
                                    combined[:, t_, hf * 512:(hf + 1) * 512],
                                    ps_y[:, t_ % 2, hf, :],
                                    wgate[:, t_, ee:ee + 1],
                                    combined[:, t_, hf * 512:(hf + 1) * 512],
                                    ALU.mult, ALU.add)
                            bguard["y%d%d" % (ts % 2, half)] = op("vector", mkev(tile, half, e),
                               waits=[mi, comb_init, wgate_done], inc="dv")
                    for j in range(G_F):
                        a1_guard[j] = C("mm")
                wug_guard[gpar] = C("mm")
                wdg_guard[gpar] = C("mm")
            if "combe" in dbg_out and tb == 0:
                dma("sync", "sq", dbg_out["combe"][e + 1], combined[:], waits=[C("dv")])
        pb_done = C("dv")

        st = dma("sync", "syo", yout[t0:t0 + TQ, :].rearrange("(t p) d -> p t d", p=128),
                 combined[:], waits=[pb_done])
        comb_store[0] = st
        xn_tb_guard[0] = C("mm")

    if "wut0" in dbg_out:
        stg = xnout[:, 0, :]
        d2 = None
        for ch in range(FF * NKT // D):
            ldb = dma("sync", "sq", stg, s_wuT[3][:, ch * D:(ch + 1) * D],
                      waits=wu_ready[3] + [d2, C("dv")] if wu_ready[0] else [d2, C("dv")])
            cv = op("vector", lambda E: E.tensor_copy(xcent, stg), waits=[ldb], inc="dv")
            d2 = dma("sync", "sq", dbg_out["wut0"][:, ch * D:(ch + 1) * D], xcent, waits=[cv])
    if "wdt0" in dbg_out:
        d2 = None
        for ch in range(FF // 128):
            ldb = dma("sync", "sq", xnout[:, 0, :], s_wdT[3][ch * 128:(ch + 1) * 128, :],
                      waits=wd_ready[3] + [d2, C("dv")] if wd_ready[0] else [d2, C("dv")])
            cv = op("vector", lambda E: E.tensor_copy(xcent, xnout[:, 0, :]), waits=[ldb], inc="dv")
            d2 = dma("sync", "sq", dbg_out["wdt0"][ch * 128:(ch + 1) * 128, :], xcent, waits=[cv])

    op("sync", lambda E: E.nop(), waits=[C(s) for s in SEMNAMES])

    nc.all_engine_barrier()
    semnums = sorted(sems[s].num for s in SEMNAMES)
    assert semnums == list(range(semnums[0], semnums[0] + len(semnums))), semnums
    clr_range = range(semnums[0], semnums[-1] + 1)
    nc.gpsimd.dma_reset(clr_range)
    nc.gpsimd.sem_clear(clr_range)
    nc.all_engine_barrier()
    with nc.Block() as block:
        def mkrunner(eng_name):
            def runner(E):
                for waits, fn, inc, k in SCHED[eng_name]:
                    for sname, val in waits:
                        E.wait_ge(sems[sname], val)
                    ins = fn(E)
                    if inc is not None:
                        ins.then_inc(sems[inc], k)
            return runner
        block.tensor(mkrunner("tensor"))
        block.vector(mkrunner("vector"))
        block.scalar(mkrunner("scalar"))
        block.gpsimd(mkrunner("gpsimd"))
        block.sync(mkrunner("sync"))
    es.close()
    return nc, dbg_out


import numpy as np
from concourse.bass_utils import run_bass_kernel_spmd

_NC_CACHE = {}

def _get_nc():
    if "nc" not in _NC_CACHE:
        _NC_CACHE["nc"] = build(dbg=())[0]
    return _NC_CACHE["nc"]


def kernel(L0_attn_x, L0_attn_Wqkv, L0_attn_Wproj, L0_ffn_gate_w, L0_ffn_gate_bias,
           L0_ffn_Wu, L0_ffn_bu, L0_ffn_Wd, L0_ffn_bd, L0_ffn_sWu, L0_ffn_sbu,
           L0_ffn_sWd, L0_ffn_sbd):
    x = np.ascontiguousarray(np.asarray(L0_attn_x, dtype=np.float32))
    com = dict(
        wqkv=np.ascontiguousarray(np.asarray(L0_attn_Wqkv, np.float32)),
        wproj=np.ascontiguousarray(np.asarray(L0_attn_Wproj, np.float32)),
        gw=np.ascontiguousarray(np.asarray(L0_ffn_gate_w, np.float32)),
        gb=np.ascontiguousarray(np.asarray(L0_ffn_gate_bias, np.float32).reshape(1, 8)),
        wu=np.ascontiguousarray(np.asarray(L0_ffn_Wu, np.float32)),
        bu=np.ascontiguousarray(np.asarray(L0_ffn_bu, np.float32)),
        wd=np.ascontiguousarray(np.asarray(L0_ffn_Wd, np.float32)),
        bd=np.ascontiguousarray(np.asarray(L0_ffn_bd, np.float32)),
        swu=np.ascontiguousarray(np.asarray(L0_ffn_sWu, np.float32)),
        sbu=np.ascontiguousarray(np.asarray(L0_ffn_sbu, np.float32).reshape(1, FF)),
        swd=np.ascontiguousarray(np.asarray(L0_ffn_sWd, np.float32)),
        sbd=np.ascontiguousarray(np.asarray(L0_ffn_sbd, np.float32).reshape(1, D)),
    )
    in_maps = []
    for c in range(8):
        b, half = c // 2, c % 2
        xb_ = x[b]
        if half == 1:
            xb_ = np.concatenate([xb_[TLOC:], xb_[:TLOC]], axis=0)
        m = dict(com)
        m["xb"] = np.ascontiguousarray(xb_)
        in_maps.append(m)
    nc = _get_nc()
    res = run_bass_kernel_spmd(nc, in_maps, core_ids=list(range(8)))
    out = np.empty((B, NSEQ, D), dtype=np.float32)
    for c in range(8):
        b, half = c // 2, c % 2
        out[b, half * TLOC:(half + 1) * TLOC, :] = res.results[c]["yout"]
    return out
